# version 11
# baseline (speedup 1.0000x reference)
import numpy as np
import ml_dtypes
import concourse.bass as bass
import concourse.mybir as mybir
from concourse.bass_utils import run_bass_kernel_spmd

F32 = mybir.dt.float32
BF16 = mybir.dt.bfloat16
AF = mybir.ActivationFunctionType
ALU = mybir.AluOpType
AX = mybir.AxisListType
bfdt = ml_dtypes.bfloat16

NCORES = 8
S, D = 8192, 2048
NH = 16
NT = 16
NQ = 8
NKT = 9
LN_EPS = 1e-5
RMS_EPS = 1e-6
THETA = 10000.0
CPC = 1.0 / 2048.0

CONSTS = [CPC, LN_EPS, RMS_EPS, 1.0 / 16.0, 128.0 * RMS_EPS, -0.2, 0.2,
          0.5, 1.0, 0.0, 0.33, -0.33]


class Ctr:
    def __init__(self, nc, name):
        self.sem = nc.alloc_semaphore(name)
        self.v = 0


class Plan:
    ENGS = ("sync", "gpsimd", "vector", "scalar", "tensor")

    def __init__(self, nc, ctrs=None):
        self.nc = nc
        self.streams = {e: [] for e in self.ENGS}
        self.last_out = {"vector": None, "scalar": None}
        self.wrote = {"vector": {}, "scalar": {}}
        self.selfwait = {"vector": 0, "scalar": 0}
        self.own = ctrs or {}

    @staticmethod
    def _rng(ap):
        if ap is None:
            return None
        try:
            import concourse.mybir as _mb
            off = ap.offset if isinstance(ap.offset, int) else -1
            esz = _mb.dt.size(ap.dtype)
            apl = list(ap.ap)
            simple = len(apl) == 2 and apl[-1][0] == 1
            if off < 0 or not simple:
                return (ap.tensor.name, -1, 0)
            return (ap.tensor.name, off * esz, apl[-1][1] * esz)
        except Exception:
            return (str(ap), -1, 0)

    def _hazard(self, eng, in_aps):
        lo = self.last_out.get(eng)
        if lo is None:
            return False
        ln, loff, lsz = lo
        for ap in in_aps:
            r = self._rng(ap)
            if r is None or r[0] != ln:
                continue
            off, sz = r[1], r[2]
            if loff < 0 or off < 0:
                return True
            if off < loff + lsz and loff < off + sz:
                return True
        return False

    def op(self, eng, fn, ins=(), out=None, sem=None, n=1):
        outs = list(out) if isinstance(out, (list, tuple)) else (
            [out] if out is not None else [])
        drains = eng in ("vector", "scalar") and self._hazard(eng, ins)
        if eng in ("vector", "scalar"):
            self.last_out[eng] = self._rng(outs[0]) if outs else None
            own = self.own[eng]
            if sem is None:
                sem, n = own, 1
            need = 0
            names = set()
            for ap in list(ins) + outs:
                r = self._rng(ap)
                if r is not None:
                    names.add(r[0])
            for nm in names:
                need = max(need, self.wrote[eng].get(nm, 0))
            if need > self.selfwait[eng]:
                self.wait(eng, own, need)
                self.selfwait[eng] = need
        if sem is not None:
            sem.v += n
        tgt = sem.v if sem is not None else None
        semh = sem.sem if sem is not None else None
        if eng in ("vector", "scalar") and outs and sem is self.own[eng]:
            for o in outs:
                r = self._rng(o)
                if r is not None:
                    self.wrote[eng][r[0]] = tgt

        def go(e, fn=fn, drains=drains, semh=semh, n=n):
            if drains:
                e.drain()
            ins_ = fn(e)
            if semh is not None:
                ins_.then_inc(semh, n)
        self.streams[eng].append(go)
        return tgt

    def dma(self, eng, dst, src, ctr):
        return self.op(eng, lambda e, d=dst, s=src: e.dma_start(out=d, in_=s),
                       sem=ctr, n=16)

    def wait(self, eng, ctr, val):
        if val is None:
            raise ValueError("wait on None")
        assert val <= ctr.v, f"wait on future value {val} > {ctr.v} ({eng})"
        self.streams[eng].append(lambda e, s=ctr.sem, v=val: e.wait_ge(s, v))

    def run(self, scope=None):
        import contextlib
        ctx = self.nc.named_scope(scope) if scope else contextlib.nullcontext()
        with ctx, self.nc.Block() as block:
            for e in self.ENGS:
                fns = self.streams[e]
                if fns:
                    getattr(block, e)(lambda eng, fns=fns: [f(eng) for f in fns])


def build_nc(kphases=99, p1lvl=4):
    nc = bass.Bass("TRN2", target_bir_lowering=False)

    x_d = nc.dram_tensor("x_all", [NT * 128, D], F32, kind="ExternalInput").ap()
    wqkv_d = nc.dram_tensor("wqkv", [D, 3 * D], F32, kind="ExternalInput").ap()
    wo_d = nc.dram_tensor("wo", [D, D], F32, kind="ExternalInput").ap()
    wfT_d = nc.dram_tensor("wfT", [D, D], F32, kind="ExternalInput").ap()
    rrT_d = nc.dram_tensor("rrT", [D, D], F32, kind="ExternalInput").ap()
    qwb_d = nc.dram_tensor("qwb", [128, D], F32, kind="ExternalInput").ap()
    kwb_d = nc.dram_tensor("kwb", [128, D], F32, kind="ExternalInput").ap()
    cos_d = nc.dram_tensor("cosT", [NT * 128, 64], F32, kind="ExternalInput").ap()
    sin_d = nc.dram_tensor("sinT", [NT * 128, 64], F32, kind="ExternalInput").ap()
    valid_d = nc.dram_tensor("valid", [128, NT], BF16, kind="ExternalInput").ap()
    y_d = nc.dram_tensor("y", [NQ * 128, D], F32, kind="ExternalOutput").ap()
    vhbm = nc.dram_tensor("vhbm", [NT, 128, D], BF16, kind="Internal").ap()

    i_np = np.arange(128)
    m0 = (i_np[None, :] < i_np[:, None]).astype(bfdt)
    m8 = (i_np[None, :] >= i_np[:, None]).astype(bfdt)
    masks_dram = nc.inline_tensor(np.concatenate([m0, m8], axis=1), name="masks_c")
    ident_dram = nc.inline_tensor(np.eye(128, dtype=bfdt), name="ident_c")
    ones_dram = nc.inline_tensor(np.ones((1, 128), dtype=bfdt), name="ones_c")
    pool_np = np.tile(np.array(CONSTS, dtype=np.float32)[None, :], (128, 1))
    pool_dram = nc.inline_tensor(pool_np, name="pool_c")

    sba = lambda n, s, d: nc.alloc_sbuf_tensor(n, s, d).ap()
    R1 = sba("R1", [128, 34816], BF16)
    R2 = sba("R2", [128, 32768], BF16)
    R3 = sba("R3", [128, 16384], BF16)
    wchunk = sba("wchunk", [128, 8192], BF16)
    wstgA = sba("wstgA", [128, 512], F32)
    wstgB = sba("wstgB", [128, 512], F32)
    consts = sba("constpool", [128, len(CONSTS)], F32)
    identb = sba("identb", [128, 128], BF16)
    masks = sba("masksb", [128, 256], BF16)
    ones = sba("onesb", [1, 128], BF16)
    qwb = sba("qwbb", [128, D], BF16)
    kwb = sba("kwbb", [128, D], BF16)
    stats = sba("stats", [128, 6 * NT], F32)
    thr = sba("thr", [128, 2 * NQ], F32)
    ropeo = sba("ropeo", [128, D], BF16)
    sa = sba("sa", [128, 1024], BF16)
    sb_ = sba("sb", [128, 1024], BF16)
    PTb = sba("PTb", [128, 2304], BF16)
    nmaxs = sba("nmaxs", [128, NH], F32)
    nmaxsb = sba("nmaxsb", [128, NH], BF16)
    nmrowA = None
    rinv = sba("rinv", [128, NH], F32)
    validb = sba("validb", [128, NT], BF16)

    for i, cval in enumerate(CONSTS):
        nc.const_aps.register(F32, cval, consts[:, i:i + 1])

    psA = nc.alloc_psum_tensor("psA", [128, 1536], F32).ap()
    psB = nc.alloc_psum_tensor("psB", [128, 1536], F32).ap()
    psC = nc.alloc_psum_tensor("psC", [128, 512], F32).ap()
    psDb = nc.alloc_psum_tensor("psD", [128, 1024], BF16).ap()
    psCb = psC.bitcast(BF16)

    xlnT = R1
    cosb = R1[:, 32768:33792]
    sinb = R1[:, 33792:34816]

    vring = R1
    o_sb_base = NKT * 2048
    kT = R2
    x2 = R2.bitcast(F32)
    qT = R3
    xqT = R3
    xres = R3.bitcast(F32)
    xtile = wchunk.bitcast(F32)
    stg32 = wchunk.bitcast(F32)
    outstg = sa.bitcast(F32)
    nmrow_half = [wstgA.bitcast(BF16), wstgB.bitcast(BF16)]
    rope_x = ropeo
    ro2stgA = ropeo.bitcast(F32)
    sums = lambda t: stats[:, 6 * t + 0:6 * t + 1]
    sumsq = lambda t: stats[:, 6 * t + 1:6 * t + 2]
    msum = lambda t: stats[:, 6 * t + 2:6 * t + 3]
    dvar = lambda t: stats[:, 6 * t + 3:6 * t + 4]
    s_of = lambda t: stats[:, 6 * t + 4:6 * t + 5]
    r_of = lambda t: stats[:, 6 * t + 5:6 * t + 6]

    WS = [Ctr(nc, "wA"), Ctr(nc, "wB"), Ctr(nc, "wC"), Ctr(nc, "wD")]
    dC = Ctr(nc, "dC")
    SP = [Ctr(nc, "spA"), Ctr(nc, "spB")]
    RNG = Ctr(nc, "rng")
    XR = [Ctr(nc, "xr0"), Ctr(nc, "xr1"), Ctr(nc, "xr2"), Ctr(nc, "xr3")]
    dO = Ctr(nc, "dO")
    WG0 = Ctr(nc, "wG0")
    WG1 = Ctr(nc, "wG1")
    ct = Ctr(nc, "ct")
    cv = Ctr(nc, "cv")
    cs = Ctr(nc, "cs")
    OWN = {"vector": cv, "scalar": cs}

    class TrPipe:
        def __init__(self):
            self.guard = [None, None]
            self.idx = 0

        def do(self, pl, src_ap, dst_ap, scale=None):
            si = self.idx % 2
            self.idx += 1
            slot = psDb[:, 0:128] if si == 0 else psCb[:, 0:128]
            if self.guard[si] is not None:
                pl.wait("tensor", cs, self.guard[si])
            tv = pl.op("tensor", lambda e, s=slot, x=src_ap: e.transpose(s, x, identb[:]),
                       sem=ct)
            pl.wait("scalar", ct, tv)
            if scale is None:
                g = pl.op("scalar", lambda e, s=slot, dd=dst_ap: e.copy(dd, s),
                          ins=[slot], out=dst_ap, sem=cs)
            else:
                g = pl.op("scalar", lambda e, s=slot, dd=dst_ap, sc=scale: e.activation(
                    out=dd, in_=s, func=AF.Copy, scale=sc), ins=[slot], out=dst_ap, sem=cs)
            self.guard[si] = g
            return g


    p = Plan(nc, OWN)
    p.dma("gpsimd", consts[:], pool_dram.ap(), dC)
    p.dma("gpsimd", identb[:], ident_dram.ap(), dC)
    p.dma("gpsimd", masks[:], masks_dram.ap(), dC)
    p.dma("gpsimd", ones[:], ones_dram.ap(), dC)
    p.dma("gpsimd", validb[:], valid_d, dC)
    dC_final = dC.v
    cos_src = bass.AP(cos_d.tensor, 0, [[64, 128], [128 * 64, 16], [1, 64]])
    sin_src = bass.AP(sin_d.tensor, 0, [[64, 128], [128 * 64, 16], [1, 64]])
    v1 = p.dma("sync", stg32[:, 0:1024], cos_src, WS[0])
    v2 = p.dma("sync", stg32[:, 1024:2048], sin_src, WS[1])
    p.wait("vector", WS[0], v1)
    vc = p.op("vector", lambda e: e.tensor_copy(out=cosb[:], in_=stg32[:, 0:1024]),
              out=cosb[:], sem=cv)
    p.wait("vector", WS[1], v2)
    p.op("vector", lambda e: e.tensor_copy(out=sinb[:], in_=stg32[:, 1024:2048]),
         out=sinb[:])
    p.wait("sync", cv, vc)
    v3 = p.dma("sync", stg32[:, 2048:4096], qwb_d, WS[0])
    p.wait("vector", WS[0], v3)
    vq = p.op("vector", lambda e: e.tensor_copy(out=qwb[:], in_=stg32[:, 2048:4096]),
              out=qwb[:], sem=cv)
    p.wait("sync", cv, vq)
    v4 = p.dma("sync", stg32[:, 2048:4096], kwb_d, WS[0])
    p.wait("vector", WS[0], v4)
    p.op("vector", lambda e: e.tensor_copy(out=kwb[:], in_=stg32[:, 2048:4096]),
         out=kwb[:])
    p.run("P0")

    def _finish_early():
        pf = Plan(nc, OWN)
        for jj in range(NQ):
            for cc in range(4):
                pf.dma("gpsimd", y_d[128 * jj:128 * (jj + 1), 512 * cc:512 * (cc + 1)],
                       x2[:, 2048 * jj + 512 * cc:2048 * jj + 512 * (cc + 1)], dO)
        pf.streams["gpsimd"].append(lambda e, v=dO.v: e.wait_ge(dO.sem, v))
        for cc in [RNG, dC, SP[0], SP[1]] + XR:
            pf.streams["gpsimd"].append(lambda e, s=cc.sem, v=cc.v: e.wait_ge(s, v))
        for cc in WS:
            pf.streams["sync"].append(lambda e, s=cc.sem, v=cc.v: e.wait_ge(s, v))
        pf.run()
        return nc
    if kphases < 1:
        return _finish_early()

    p = Plan(nc, OWN)
    for engx in ("tensor", "scalar", "vector"):
        p.wait(engx, dC, dC_final)
    tr = TrPipe()
    xslot = [xtile[:, 0:2048], xtile[:, 2048:4096]]
    cs_apply = {}
    last_tr_of_tile = {}
    dump = PTb[:, 0:2048]
    for t in range(NT):
        if t >= 2 and p1lvl >= 3:
            p.wait("sync", cs, cs_apply[t - 2])
        ld = p.dma("sync", xslot[t % 2], x_d[128 * t:128 * (t + 1), :], WS[t % 2])
        xs_ = xslot[t % 2]
        if p1lvl < 1:
            continue
        p.wait("scalar", WS[t % 2], ld)
        p.op("scalar", lambda e, x=xs_, a=sums(t): e.activation(
            out=dump, in_=x, func=AF.Copy, accum_out=a), ins=[xs_], out=(dump, sums(t)))
        st1 = p.op("scalar", lambda e, x=xs_, t=t: e.activation(
            out=dump, in_=x, func=AF.Square, accum_out=sumsq(t)),
            ins=[xs_], out=(dump, sumsq(t)), sem=cs)
        if p1lvl < 2:
            continue
        p.wait("vector", cs, st1)
        p.op("vector", lambda e, t=t: e.scalar_tensor_tensor(
            msum(t), sums(t), CPC, sums(t), ALU.mult, ALU.mult),
            ins=[sums(t)], out=msum(t))
        cvd = p.op("vector", lambda e, t=t: e.scalar_tensor_tensor(
            dvar(t), msum(t), -1.0, sumsq(t), ALU.mult, ALU.add),
            ins=[msum(t), sumsq(t)], out=dvar(t), sem=cv)
        p.wait("scalar", cv, cvd)
        css = p.op("scalar", lambda e, t=t: e.activation(
            out=s_of(t), in_=dvar(t), func=AF.Sqrt, scale=CPC, bias=LN_EPS),
            ins=[dvar(t)], out=s_of(t), sem=cs)
        p.wait("vector", cs, css)
        p.op("vector", lambda e, t=t: e.reciprocal(r_of(t), s_of(t)),
             ins=[s_of(t)], out=r_of(t))
        cvr = p.op("vector", lambda e, t=t: e.scalar_tensor_tensor(
            msum(t), sums(t), -CPC, r_of(t), ALU.mult, ALU.mult),
            ins=[sums(t), r_of(t)], out=msum(t), sem=cv)
        if p1lvl < 3:
            continue
        p.wait("scalar", cv, cvr)
        if t >= 1:
            p.wait("scalar", ct, last_tr_of_tile[t - 1])
        cs_apply[t] = p.op("scalar", lambda e, t=t, x=xs_: e.activation(
            out=ropeo[:], in_=x, func=AF.Identity, scale=r_of(t), bias=msum(t)),
            ins=[xs_, r_of(t), msum(t)], out=ropeo[:], sem=cs)
        if p1lvl < 4:
            last_tr_of_tile[t] = ct.v
            continue
        p.wait("tensor", cs, cs_apply[t])
        for d in range(16):
            dst = xlnT[:, 128 * (16 * t + d):128 * (16 * t + d + 1)]
            tr.do(p, ropeo[:, 128 * d:128 * (d + 1)], dst)
        last_tr_of_tile[t] = ct.v
    p.run("P1")
    if kphases < 2:
        return _finish_early()

    def qkv_pass(pl, col0, nchunks, tiles, dst_fn, pre_evac=None, post_evac=None,
                 w_dram=None, lhs_col=None):
        if w_dram is None:
            w_dram = wqkv_d
        if lhs_col is None:
            lhs_col = lambda t, d: 128 * (16 * t + d)
        ev = {}
        stg_guard = [None, None]
        par_guard = [None, None]
        prev_mm = None
        for ci in range(nchunks):
            c0 = col0 + 512 * ci
            if prev_mm is not None:
                pl.wait("vector", ct, prev_mm)
            for d in range(16):
                si = d % 2
                stg = wstgA if si == 0 else wstgB
                qeng = "sync" if si == 0 else "gpsimd"
                qsem = WS[0] if si == 0 else WG0
                if stg_guard[si] is not None:
                    pl.wait(qeng, cv, stg_guard[si])
                ld = pl.dma(qeng, stg[:], w_dram[128 * d:128 * (d + 1), c0:c0 + 512],
                            qsem)
                pl.wait("vector", qsem, ld)
                dstw = wchunk[:, 512 * d:512 * (d + 1)]
                stg_guard[si] = pl.op("vector", lambda e, s=stg, w=dstw: e.tensor_copy(
                    out=w, in_=s), ins=[stg[:]], out=dstw, sem=cv)
            cast15 = stg_guard[1]
            pl.wait("tensor", cv, cast15)
            for ti, t in enumerate(tiles):
                par = ti % 2
                ps = (psA if par == 0 else psB)[:, 0:512]
                if par_guard[par] is not None:
                    pl.wait("tensor", cs, par_guard[par])
                ctm = None
                for d in range(16):
                    ctm = pl.op("tensor", lambda e, d=d, t=t, ps=ps: e.matmul(
                        ps, xlnT[:, lhs_col(t, d):lhs_col(t, d) + 128],
                        wchunk[:, 512 * d:512 * (d + 1)],
                        start=(d == 0), stop=(d == 15)),
                        sem=(ct if d == 15 else None))
                pl.wait("scalar", ct, ctm)
                if pre_evac is not None:
                    pre_evac(pl, t, ci)
                dst = dst_fn(t, ci)
                ev[(t, ci)] = pl.op("scalar", lambda e, ps=ps, dd=dst: e.copy(dd, ps),
                                    ins=[ps], out=dst, sem=cs)
                par_guard[par] = ev[(t, ci)]
                if post_evac is not None:
                    post_evac(pl, t, ci, ev[(t, ci)])
            prev_mm = ctm
        return ev

    def postprocess(pl, region, tiles, blkstride, base_of, w_b, sqscale, sqbias,
                    ev, last_ci, tq_of):
        rowstride = region.tensor.shape[1]

        def rows(t):
            return bass.AP(region.tensor, 128 * base_of(t),
                           [[rowstride, 128], [blkstride, 16], [1, 128]])

        def half(t, hh):
            return bass.AP(region.tensor, 128 * base_of(t) + 64 * hh,
                           [[rowstride, 128], [blkstride, 16], [1, 64]])

        for t in tiles:
            pl.wait("scalar", cs, ev[(t, last_ci)])
            pl.op("scalar", lambda e, s=rows(t), t=t: e.activation(
                out=dump, in_=s, func=AF.Square, accum_out=sumsq(t)),
                ins=[rows(t)], out=(dump, sumsq(t)))
        for t in tiles:
            pl.op("scalar", lambda e, t=t: e.activation(
                out=s_of(t), in_=sumsq(t), func=AF.Sqrt, scale=sqscale, bias=sqbias),
                ins=[sumsq(t)], out=s_of(t), sem=cs)
        cs_sqrt = cs.v
        pl.wait("vector", cs, cs_sqrt)
        for t in tiles:
            pl.op("vector", lambda e, t=t: e.reciprocal(r_of(t), s_of(t)),
                  ins=[s_of(t)], out=r_of(t), sem=cv)
        cv_r = cv.v
        pl.wait("scalar", cv, cv_r)
        for t in tiles:
            pl.op("scalar", lambda e, s=rows(t), t=t: e.activation(
                out=s, in_=s, func=AF.Copy, scale=r_of(t)),
                ins=[rows(t), r_of(t)], out=rows(t), sem=cs)
        cs_ap = cs.v
        pl.wait("vector", cs, cs_ap)
        wap = bass.AP(w_b.tensor, 0, [[2048, 128], [128, 16], [1, 128]])
        for t in tiles:
            pl.op("vector", lambda e, s=rows(t): e.tensor_mul(s, s, wap),
                  ins=[rows(t)], out=rows(t), sem=cv)
        cv_w = cv.v
        pl.wait("vector", cv, cv_w)
        tr2 = TrPipe()
        prev_tr = None
        for t in tiles:
            x1 = half(t, 0)
            x2_ = half(t, 1)
            ro1 = bass.AP(ropeo.tensor, 0, [[2048, 128], [128, 16], [1, 64]])
            ro2 = bass.AP(ropeo.tensor, 64, [[2048, 128], [128, 16], [1, 64]])
            sa_a = bass.AP(sa.tensor, 0, [[1024, 128], [64, 16], [1, 64]])
            sb_a = bass.AP(sb_.tensor, 0, [[1024, 128], [64, 16], [1, 64]])
            c64 = cosb[:, 64 * t:64 * t + 64]
            s64 = sinb[:, 64 * t:64 * t + 64]
            cos_a = bass.AP(c64.tensor, c64.offset,
                            [list(c64.ap[0]), [0, 16], list(c64.ap[1])])
            sin_a = bass.AP(s64.tensor, s64.offset,
                            [list(s64.ap[0]), [0, 16], list(s64.ap[1])])
            if prev_tr is not None:
                pl.wait("vector", ct, prev_tr)
            pl.op("vector", lambda e, a=ro1, b=x1, c=cos_a: e.tensor_mul(a, b, c),
                  ins=[x1], out=ro1)
            pl.op("vector", lambda e, a=sa_a, b=x2_, c=sin_a: e.tensor_mul(a, b, c),
                  ins=[x2_], out=sa_a)
            pl.op("vector", lambda e, a=ro2, b=x2_, c=cos_a: e.tensor_mul(a, b, c),
                  ins=[x2_], out=ro2)
            pl.op("vector", lambda e, a=sb_a, b=x1, c=sin_a: e.tensor_mul(a, b, c),
                  ins=[x1], out=sb_a)
            pl.op("vector", lambda e, a=ro1, b=sa_a: e.tensor_sub(a, a, b),
                  ins=[ro1, sa_a], out=ro1)
            cvrope = pl.op("vector", lambda e, a=ro2, b=sb_a: e.tensor_add(a, a, b),
                           ins=[ro2, sb_a], out=ro2, sem=cv)
            pl.wait("tensor", cv, cvrope)
            for h in range(16):
                dst = bass.AP(region.tensor, blkstride * h + 128 * tq_of(t),
                              [[rowstride, 128], [1, 128]])
                tr2.do(pl, ropeo[:, 128 * h:128 * (h + 1)], dst)
            prev_tr = ct.v

    p = Plan(nc, OWN)

    def k_dst(t, ci):
        return bass.AP(kT.tensor, 2048 * (4 * ci) + 128 * t,
                       [[32768, 128], [2048, 4], [1, 128]])

    ev_k = qkv_pass(p, 2048, 4, list(range(NT)), k_dst)
    postprocess(p, kT, list(range(NT)), 2048, lambda t: t, kwb, CPC, RMS_EPS,
                ev_k, 3, tq_of=lambda t: t)
    p.run("P2k")
    if kphases < 3:
        return _finish_early()

    p = Plan(nc, OWN)

    def q_dst(t, ci):
        return bass.AP(qT.tensor, 1024 * (4 * ci) + 128 * (t - 8),
                       [[16384, 128], [1024, 4], [1, 128]])

    own = list(range(8, 16))
    ev_q = qkv_pass(p, 0, 4, own, q_dst)
    postprocess(p, qT, own, 1024, lambda t: t - 8, qwb, 1.0 / 16.0, 128.0 * RMS_EPS,
                ev_q, 3, tq_of=lambda t: t - 8)
    p.run("P3q")
    if kphases < 4:
        return _finish_early()

    p = Plan(nc, OWN)
    vstg = [PTb[:, 0:512], PTb[:, 512:1024]]
    spill_guard = [None, None]

    def v_dst(t, ci):
        return vstg[(16 * ci + t) % 2][:]

    def v_pre(pl, t, ci):
        si = (16 * ci + t) % 2
        if spill_guard[si] is not None:
            pl.wait("scalar", SP[si], spill_guard[si])

    def v_post(pl, t, ci, evv):
        si = (16 * ci + t) % 2
        pl.wait("gpsimd", cs, evv)
        spill_guard[si] = pl.dma(
            "gpsimd", bass.AP(vhbm.tensor, t * 128 * D + 512 * ci, [[D, 128], [1, 512]]),
            vstg[si][:], SP[si])

    qkv_pass(p, 4096, 4, list(range(NT)), v_dst, pre_evac=v_pre, post_evac=v_post)
    p.run("P4v")
    if kphases < 5:
        return _finish_early()

    p = Plan(nc, OWN)
    p.wait("gpsimd", SP[0], SP[0].v)
    p.wait("gpsimd", SP[1], SP[1].v)
    ring_ld = {}
    for l in range(NKT):
        ring_ld[l] = p.dma("gpsimd",
                           vring[:, 2048 * (l % NKT):2048 * (l % NKT) + 2048],
                           bass.AP(vhbm.tensor, l * 128 * D, [[D, 128], [1, 2048]]), RNG)
    bank_guard = [None, None]
    pt_guard = [None, None]
    psc_guard = None
    ring_guard = {}
    o_col = lambda j, h: o_sb_base + 2048 * j + 128 * h
    for j in range(NQ):
        if j >= 1:
            sl = (j + 8) % NKT
            p.wait("gpsimd", ct, ring_guard[j - 1])
            ring_ld[j + 8] = p.dma(
                "gpsimd", vring[:, 2048 * sl:2048 * sl + 2048],
                bass.AP(vhbm.tensor, (j + 8) * 128 * D, [[D, 128], [1, 2048]]), RNG)
        red = {}
        for h in range(NH):
            par = h % 2
            ps = (psA if par == 0 else psB)
            if bank_guard[par] is not None:
                c, v = bank_guard[par]
                p.wait("tensor", c, v)
            q_ap = qT[:, 1024 * h + 128 * j:1024 * h + 128 * (j + 1)]
            ctm = None
            for (off, n) in ((0, 512), (512, 512), (1024, 128)):
                kbase = 2048 * h + 128 * j
                ctm = p.op("tensor", lambda e, ps=ps, q=q_ap, kb=kbase, o=off, n=n: e.matmul(
                    ps[:, o:o + n], q, kT[:, kb + o:kb + o + n], start=True, stop=True),
                    sem=(ct if off == 1024 else None))
            p.wait("vector", ct, ctm)
            red[h] = p.op("vector", lambda e, ps=ps, h=h: e.tensor_reduce(
                out=nmaxs[:, h:h + 1], in_=ps[:, 0:1152], axis=AX.X, op=ALU.max,
                negate=True), ins=[ps[:, 0:1152]], out=nmaxs[:, h:h + 1], sem=cv)
            bank_guard[par] = (cv, red[h])
        p.op("vector", lambda e: e.tensor_copy(out=nmaxsb[:, 0:8], in_=nmaxs[:, 0:8]),
             ins=[nmaxs[:, 0:8]], out=nmaxsb[:, 0:8])
        castb = p.op("vector", lambda e: e.tensor_copy(out=nmaxsb[:, 8:16],
                                                       in_=nmaxs[:, 8:16]),
                     ins=[nmaxs[:, 8:16]], out=nmaxsb[:, 8:16], sem=cv)
        p.wait("tensor", cv, castb)
        nm = {}
        for b in range(2):
            if b == 1:
                p.wait("tensor", cs, nm[0])
            elif j >= 1:
                p.wait("tensor", cs, nm2_prev)
            tv = None
            for h in range(8 * b, 8 * b + 8):
                tv = p.op("tensor", lambda e, h=h, b=b: e.transpose(
                    psDb[0:1, 128 * (h - 8 * b):128 * (h - 8 * b + 1)],
                    nmaxsb[:, h:h + 1], identb[:]), sem=(ct if h == 8 * b + 7 else None))
            p.wait("scalar", ct, tv)
            nm[b] = p.op("scalar", lambda e, b=b: e.copy(
                nmrow_half[b][0:1, 0:1024], psDb[0:1, 0:1024]),
                ins=[psDb[0:1, 0:1024]], out=nmrow_half[b][0:1, 0:1024], sem=cs)
        nm2 = nm[1]
        nm2_prev = nm2
        p.wait("tensor", RNG, ring_ld[j + 8])
        mk_of = {}

        def emit_ST(h):
            par = h % 2
            ps = psA if par == 0 else psB
            if bank_guard[par] is not None:
                c, v = bank_guard[par]
                p.wait("tensor", c, v)
            if h == 0:
                p.wait("tensor", cs, nm2)
            kbase = 2048 * h + 128 * j
            q_ap = qT[:, 1024 * h + 128 * j:1024 * h + 128 * (j + 1)]
            ctst = None
            for l in range(NKT):
                p.op("tensor", lambda e, ps=ps, l=l, kb=kbase, q=q_ap: e.matmul(
                    ps[:, 128 * l:128 * (l + 1)],
                    kT[:, kb + 128 * l:kb + 128 * (l + 1)],
                    q, start=True, stop=False))
                nmr = nmrow_half[h // 8][0:1, 128 * (h % 8):128 * (h % 8 + 1)]
                ctst = p.op("tensor", lambda e, ps=ps, l=l, r=nmr: e.matmul(
                    ps[:, 128 * l:128 * (l + 1)], ones[:], r, start=False, stop=True),
                    sem=(ct if l == NKT - 1 else None))
            PT = PTb[:, 1152 * par:1152 * (par + 1)]
            p.wait("scalar", ct, ctst)
            if pt_guard[par] is not None:
                c, v = pt_guard[par]
                p.wait("scalar", c, v)
            expv = p.op("scalar", lambda e, ps=ps, PT=PT: e.activation(
                out=PT, in_=ps[:, 0:1152], func=AF.Exp), ins=[ps[:, 0:1152]], out=PT,
                sem=cs)
            bank_guard[par] = (cs, expv)
            p.wait("vector", cs, expv)
            p.op("vector", lambda e, PT=PT: e.tensor_mul(
                PT[:, 0:128], PT[:, 0:128], masks[:, 0:128]),
                ins=[PT[:, 0:128]], out=PT[:, 0:128])
            mk_of[h] = p.op("vector", lambda e, PT=PT: e.tensor_mul(
                PT[:, 1024:1152], PT[:, 1024:1152], masks[:, 128:256]),
                ins=[PT[:, 1024:1152]], out=PT[:, 1024:1152], sem=cv)

        def emit_PV(h):
            nonlocal psc_guard
            par = h % 2
            PT = PTb[:, 1152 * par:1152 * (par + 1)]
            p.wait("tensor", cv, mk_of[h])
            if psc_guard is not None:
                p.wait("tensor", cs, psc_guard)
            for l in range(NKT):
                sl = (j + l) % NKT
                p.op("tensor", lambda e, PT=PT, l=l, sl=sl, h=h: e.matmul(
                    psC[:, 0:128], PT[:, 128 * l:128 * (l + 1)],
                    vring[:, 2048 * sl + 128 * h:2048 * sl + 128 * (h + 1)],
                    start=(l == 0), stop=(l == NKT - 1)))
            ctpv = None
            for l in range(NKT):
                ctpv = p.op("tensor", lambda e, PT=PT, l=l, j=j: e.matmul(
                    psC[:, 128:129], PT[:, 128 * l:128 * (l + 1)],
                    validb[:, j + l:j + l + 1], start=(l == 0), stop=(l == NKT - 1)),
                    sem=(ct if l == NKT - 1 else None))
            pt_guard[par] = (ct, ctpv)
            p.wait("vector", ct, ctpv)
            rcv = p.op("vector", lambda e, h=h: e.reciprocal(
                rinv[:, h:h + 1], psC[:, 128:129]), ins=[psC[:, 128:129]],
                out=rinv[:, h:h + 1], sem=cv)
            p.wait("scalar", cv, rcv)
            oc = o_col(j, h)
            psc_guard = p.op("scalar", lambda e, h=h, oc=oc: e.activation(
                out=R1[:, oc:oc + 128], in_=psC[:, 0:128], func=AF.Copy,
                scale=rinv[:, h:h + 1]), ins=[psC[:, 0:128], rinv[:, h:h + 1]],
                out=R1[:, oc:oc + 128], sem=cs)

        emit_ST(0)
        for h in range(1, NH):
            emit_ST(h)
            emit_PV(h - 1)
        emit_PV(NH - 1)
        ring_guard[j] = ct.v
    p.run("P5attn")
    if kphases < 6:
        return _finish_early()

    p = Plan(nc, OWN)
    tr6 = TrPipe()
    for j in range(NQ):
        for h in range(NH):
            c0 = o_sb_base + 2048 * j + 128 * h
            tr6.do(p, R1[:, c0:c0 + 128], R1[:, c0:c0 + 128])
    cs_oT = cs.v
    p.wait("tensor", cs, cs_oT)
    xres_guard = [None, None, None, None]
    stg_guard6 = [None, None]
    par_guard6 = [None, None]
    prev_mm6 = None
    for ci in range(4):
        c0 = 512 * ci
        if prev_mm6 is not None:
            p.wait("vector", ct, prev_mm6)
        for d in range(16):
            si = d % 2
            stg = wstgA if si == 0 else wstgB
            qeng = "sync" if si == 0 else "gpsimd"
            qsem = WS[0] if si == 0 else WG0
            if stg_guard6[si] is not None:
                p.wait(qeng, cv, stg_guard6[si])
            ld = p.dma(qeng, stg[:], wo_d[128 * d:128 * (d + 1), c0:c0 + 512], qsem)
            p.wait("vector", qsem, ld)
            dstw = wchunk[:, 512 * d:512 * (d + 1)]
            stg_guard6[si] = p.op("vector", lambda e, s=stg, w=dstw: e.tensor_copy(
                out=w, in_=s), ins=[stg[:]], out=dstw, sem=cv)
        p.wait("tensor", cv, stg_guard6[1])
        for j in range(NQ):
            si = j % 4
            ps = (psA if j % 2 == 0 else psB)[:, 0:512]
            if xres_guard[si] is not None:
                p.wait("gpsimd", cv, xres_guard[si])
            xl = p.dma("gpsimd", xres[:, 512 * si:512 * (si + 1)],
                       x_d[128 * (8 + j):128 * (9 + j), c0:c0 + 512], XR[si])
            if par_guard6[j % 2] is not None:
                p.wait("tensor", cv, par_guard6[j % 2])
            ctm = None
            for h in range(NH):
                c0o = o_sb_base + 2048 * j + 128 * h
                ctm = p.op("tensor", lambda e, ps=ps, c0o=c0o, h=h: e.matmul(
                    ps, R1[:, c0o:c0o + 128], wchunk[:, 512 * h:512 * (h + 1)],
                    start=(h == 0), stop=(h == NH - 1)),
                    sem=(ct if h == NH - 1 else None))
            p.wait("vector", ct, ctm)
            p.wait("vector", XR[si], xl)
            dstx = x2[:, 2048 * j + c0:2048 * j + c0 + 512]
            evj = p.op("vector", lambda e, ps=ps, si=si, dd=dstx: e.tensor_add(
                dd, ps, xres[:, 512 * si:512 * (si + 1)]),
                ins=[ps, xres[:, 512 * si:512 * (si + 1)]], out=dstx, sem=cv)
            xres_guard[si] = evj
            par_guard6[j % 2] = evj
        prev_mm6 = ctm
    p.run("P6wo")
    if kphases < 7:
        return _finish_early()

    p = Plan(nc, OWN)
    th_of = lambda j: thr[:, 2 * j:2 * j + 1]
    tl_of = lambda j: thr[:, 2 * j + 1:2 * j + 2]
    tr7 = TrPipe()
    cvx_prev = None
    for j in range(NQ):
        xt = x2[:, 2048 * j:2048 * (j + 1)]
        if cvx_prev is not None:
            p.wait("scalar", cv, cvx_prev)
        p.op("scalar", lambda e, x=xt, a=sums(j): e.activation(
            out=dump, in_=x, func=AF.Copy, accum_out=a), ins=[xt], out=(dump, sums(j)))
        stv = p.op("scalar", lambda e, x=xt, a=sumsq(j): e.activation(
            out=dump, in_=x, func=AF.Square, accum_out=a), ins=[xt],
            out=(dump, sumsq(j)), sem=cs)
        p.wait("vector", cs, stv)
        p.op("vector", lambda e, j=j: e.scalar_tensor_tensor(
            msum(j), sums(j), CPC, sums(j), ALU.mult, ALU.mult),
            ins=[sums(j)], out=msum(j))
        cvd = p.op("vector", lambda e, j=j: e.scalar_tensor_tensor(
            dvar(j), msum(j), -1.0, sumsq(j), ALU.mult, ALU.add),
            ins=[msum(j), sumsq(j)], out=dvar(j), sem=cv)
        p.wait("scalar", cv, cvd)
        csq = p.op("scalar", lambda e, j=j: e.activation(
            out=s_of(j), in_=dvar(j), func=AF.Sqrt, scale=CPC, bias=LN_EPS),
            ins=[dvar(j)], out=s_of(j), sem=cs)
        p.wait("vector", cs, csq)
        p.op("vector", lambda e, j=j: e.tensor_scalar_mul(r_of(j), sums(j), CPC),
             ins=[sums(j)], out=r_of(j))
        p.op("vector", lambda e, j=j: e.scalar_tensor_tensor(
            th_of(j), s_of(j), 0.33, r_of(j), ALU.mult, ALU.add),
            ins=[s_of(j), r_of(j)], out=th_of(j))
        cvt = p.op("vector", lambda e, j=j: e.scalar_tensor_tensor(
            tl_of(j), s_of(j), -0.33, r_of(j), ALU.mult, ALU.add),
            ins=[s_of(j), r_of(j)], out=tl_of(j), sem=cv)
        p.wait("vector", cv, cvt)
        g_d = PTb[:, 0:2048]
        l_d = ropeo[:, 0:2048]
        p.op("vector", lambda e, x=xt, j=j: e.tensor_single_scalar(
            g_d, x, th_of(j), ALU.is_gt), ins=[xt, th_of(j)], out=g_d)
        p.op("vector", lambda e, x=xt, j=j: e.tensor_single_scalar(
            l_d, x, tl_of(j), ALU.is_lt), ins=[xt, tl_of(j)], out=l_d)
        xq_t = xqT[:, 2048 * j:2048 * (j + 1)]
        cvx = p.op("vector", lambda e, a=xq_t, g=g_d, l=l_d: e.tensor_sub(a, g, l),
                   ins=[g_d, l_d], out=xq_t, sem=cv)
        cvx_prev = cvx
        p.wait("tensor", cv, cvx)
        for d in range(16):
            c0 = 2048 * j + 128 * d
            tr7.do(p, xqT[:, c0:c0 + 128], xqT[:, c0:c0 + 128], scale=0.5)
    cs_xqT = cs.v
    p.run("P7ln2")
    if kphases < 8:
        return _finish_early()

    p = Plan(nc, OWN)
    tmpA = [PTb[:, 0:512], PTb[:, 1024:1536]]
    tmpB = [PTb[:, 512:1024], PTb[:, 1536:2048]]
    stgW = [wstgA, ro2stgA[:, 0:512]]
    stgR = [wstgB, ro2stgA[:, 512:1024]]
    out_guard = None
    built_guard = [None, None]
    par_guard8 = [None, None]
    prev_mm8 = None
    for ci in range(4):
        c0 = 512 * ci
        if prev_mm8 is not None:
            p.wait("vector", ct, prev_mm8)
        for d in range(16):
            si = d % 2
            rsem = WG0 if si == 0 else WG1
            if built_guard[si] is not None:
                p.wait("sync", cv, built_guard[si])
                p.wait("gpsimd", cv, built_guard[si])
                p.wait("scalar", cv, built_guard[si])
            ldw = p.dma("sync", stgW[si][:], wfT_d[128 * d:128 * (d + 1), c0:c0 + 512],
                        WS[si])
            ldr = p.dma("gpsimd", stgR[si][:], rrT_d[128 * d:128 * (d + 1), c0:c0 + 512],
                        rsem)
            p.wait("scalar", WS[si], ldw)
            p.op("scalar", lambda e, s=stgW[si], tA=tmpA[si]: e.activation(
                out=tA, in_=s[:], func=AF.Sign, bias=-0.2), ins=[stgW[si][:]], out=tmpA[si])
            sgn = p.op("scalar", lambda e, s=stgW[si], tB=tmpB[si]: e.activation(
                out=tB, in_=s[:], func=AF.Sign, bias=0.2), ins=[stgW[si][:]], out=tmpB[si],
                sem=cs)
            p.wait("vector", cs, sgn)
            p.op("vector", lambda e, tA=tmpA[si], tB=tmpB[si]: e.tensor_add(tA, tA, tB),
                 ins=[tmpA[si], tmpB[si]], out=tmpA[si])
            p.wait("vector", rsem, ldr)
            dstw = wchunk[:, 512 * d:512 * (d + 1)]
            built_guard[si] = p.op(
                "vector", lambda e, s=stgR[si], w=dstw, tA=tmpA[si]: e.scalar_tensor_tensor(
                    w, s[:], 3.3333333, tA, ALU.mult, ALU.add),
                ins=[stgR[si][:], tmpA[si]], out=dstw, sem=cv)
        p.wait("tensor", cv, built_guard[1])
        if ci == 0:
            p.wait("tensor", cs, cs_xqT)
        for j in range(NQ):
            ps = (psA if j % 2 == 0 else psB)[:, 0:512]
            if par_guard8[j % 2] is not None:
                p.wait("tensor", cv, par_guard8[j % 2])
            ctm = None
            for d in range(16):
                c0x = 2048 * j + 128 * d
                ctm = p.op("tensor", lambda e, ps=ps, c0x=c0x, d=d: e.matmul(
                    ps, xqT[:, c0x:c0x + 128], wchunk[:, 512 * d:512 * (d + 1)],
                    start=(d == 0), stop=(d == 15)),
                    sem=(ct if d == 15 else None))
            p.wait("vector", ct, ctm)
            if out_guard is not None:
                p.wait("vector", dO, out_guard)
            dstx = x2[:, 2048 * j + c0:2048 * j + c0 + 512]
            evo = p.op("vector", lambda e, ps=ps, xx=dstx: e.scalar_tensor_tensor(
                outstg[:, 0:512], ps, 0.3, xx, ALU.mult, ALU.add),
                ins=[ps, dstx], out=outstg[:, 0:512], sem=cv)
            par_guard8[j % 2] = evo
            p.wait("gpsimd", cv, evo)
            out_guard = p.dma("gpsimd", y_d[128 * j:128 * (j + 1), c0:c0 + 512],
                              outstg[:, 0:512], dO)
        prev_mm8 = ctm
    for cc in [dO, RNG, dC, SP[0], SP[1], WG0, WG1] + XR:
        p.streams["gpsimd"].append(lambda e, s=cc.sem, v=cc.v: e.wait_ge(s, v))
    for cc in WS:
        p.streams["sync"].append(lambda e, s=cc.sem, v=cc.v: e.wait_ge(s, v))
    p.run("P8ffn")

    return nc


_NC_CACHE = {}


def kernel(L0_attn_hidden_states, L0_attn_Wqkv, L0_attn_q_norm_w, L0_attn_k_norm_w,
           L0_attn_Wo, L0_ffn_weight, L0_ffn_rram_variation):
    X = np.ascontiguousarray(np.asarray(L0_attn_hidden_states, dtype=np.float32)
                             .reshape(S, D))
    wqkv = np.ascontiguousarray(np.asarray(L0_attn_Wqkv, dtype=np.float32))
    wo = np.ascontiguousarray(np.asarray(L0_attn_Wo, dtype=np.float32))
    qw = np.asarray(L0_attn_q_norm_w, dtype=np.float32)
    kw = np.asarray(L0_attn_k_norm_w, dtype=np.float32)
    wfT = np.ascontiguousarray(np.asarray(L0_ffn_weight, dtype=np.float32).T)
    rrT = np.ascontiguousarray(np.asarray(L0_ffn_rram_variation, dtype=np.float32).T)

    qwb = np.ascontiguousarray(np.broadcast_to(qw[None, :], (128, D)))
    kwb = np.ascontiguousarray(np.broadcast_to(kw[None, :], (128, D)))

    inv_freq = 1.0 / THETA ** (np.arange(0, 64, dtype=np.float64) / 64.0)

    if "nc" not in _NC_CACHE:
        _NC_CACHE["nc"] = build_nc()
    nc = _NC_CACHE["nc"]

    in_maps = []
    for c in range(NCORES):
        t0 = 1024 * c - 1024
        if c == 0:
            x_all = np.concatenate([np.zeros((1024, D), np.float32),
                                    X[0:1024]], axis=0)
        else:
            x_all = X[t0:t0 + 2048]
        pos = np.arange(t0, t0 + 2048, dtype=np.float64)
        pos = np.clip(pos, 0, None)
        fr = pos[:, None] * inv_freq[None, :]
        cosT = np.cos(fr).astype(np.float32)
        sinT = np.sin(fr).astype(np.float32)
        valid = np.ones((128, NT), dtype=bfdt)
        if c == 0:
            valid[:, 0:8] = 0
        in_maps.append({
            "x_all": np.ascontiguousarray(x_all),
            "wqkv": wqkv, "wo": wo, "wfT": wfT, "rrT": rrT,
            "qwb": qwb, "kwb": kwb,
            "cosT": cosT, "sinT": sinT, "valid": valid,
        })

    res = run_bass_kernel_spmd(nc, in_maps, core_ids=list(range(NCORES)))
    out = np.concatenate([res.results[c]["y"] for c in range(NCORES)], axis=0)
    return out.reshape(1, S, D)


# version 12
# speedup vs baseline: 1.0475x; 1.0475x over previous
import numpy as np
import ml_dtypes
import concourse.bass as bass
import concourse.mybir as mybir
from concourse.bass_utils import run_bass_kernel_spmd

F32 = mybir.dt.float32
BF16 = mybir.dt.bfloat16
AF = mybir.ActivationFunctionType
ALU = mybir.AluOpType
AX = mybir.AxisListType
bfdt = ml_dtypes.bfloat16

NCORES = 8
S, D = 8192, 2048
NH = 16
NT = 16
NQ = 8
NKT = 9
LN_EPS = 1e-5
RMS_EPS = 1e-6
THETA = 10000.0
CPC = 1.0 / 2048.0

CONSTS = [CPC, LN_EPS, RMS_EPS, 1.0 / 16.0, 128.0 * RMS_EPS, -0.2, 0.2,
          0.5, 1.0, 0.0, 0.33, -0.33]


class Ctr:
    def __init__(self, nc, name):
        self.sem = nc.alloc_semaphore(name)
        self.v = 0


class Plan:
    ENGS = ("sync", "gpsimd", "vector", "scalar", "tensor")

    def __init__(self, nc, ctrs=None):
        self.nc = nc
        self.streams = {e: [] for e in self.ENGS}
        self.last_out = {"vector": None, "scalar": None}
        self.wrote = {"vector": {}, "scalar": {}}
        self.selfwait = {"vector": 0, "scalar": 0}
        self.own = ctrs or {}

    @staticmethod
    def _rng(ap):
        if ap is None:
            return None
        try:
            import concourse.mybir as _mb
            off = ap.offset if isinstance(ap.offset, int) else -1
            esz = _mb.dt.size(ap.dtype)
            apl = list(ap.ap)
            simple = len(apl) == 2 and apl[-1][0] == 1
            if off < 0 or not simple:
                return (ap.tensor.name, -1, 0)
            return (ap.tensor.name, off * esz, apl[-1][1] * esz)
        except Exception:
            return (str(ap), -1, 0)

    def _hazard(self, eng, in_aps):
        lo = self.last_out.get(eng)
        if lo is None:
            return False
        ln, loff, lsz = lo
        for ap in in_aps:
            r = self._rng(ap)
            if r is None or r[0] != ln:
                continue
            off, sz = r[1], r[2]
            if loff < 0 or off < 0:
                return True
            if off < loff + lsz and loff < off + sz:
                return True
        return False

    def op(self, eng, fn, ins=(), out=None, sem=None, n=1):
        outs = list(out) if isinstance(out, (list, tuple)) else (
            [out] if out is not None else [])
        drains = eng in ("vector", "scalar") and self._hazard(eng, ins)
        if eng in ("vector", "scalar"):
            self.last_out[eng] = self._rng(outs[0]) if outs else None
            own = self.own[eng]
            if sem is None:
                sem, n = own, 1
            need = 0
            names = set()
            for ap in list(ins) + outs:
                r = self._rng(ap)
                if r is not None:
                    names.add(r[0])
            for nm in names:
                need = max(need, self.wrote[eng].get(nm, 0))
            if need > self.selfwait[eng]:
                self.wait(eng, own, need)
                self.selfwait[eng] = need
        if sem is not None:
            sem.v += n
        tgt = sem.v if sem is not None else None
        semh = sem.sem if sem is not None else None
        if eng in ("vector", "scalar") and outs and sem is self.own[eng]:
            for o in outs:
                r = self._rng(o)
                if r is not None:
                    self.wrote[eng][r[0]] = tgt

        def go(e, fn=fn, drains=drains, semh=semh, n=n):
            if drains:
                e.drain()
            ins_ = fn(e)
            if semh is not None:
                ins_.then_inc(semh, n)
        self.streams[eng].append(go)
        return tgt

    def dma(self, eng, dst, src, ctr):
        return self.op(eng, lambda e, d=dst, s=src: e.dma_start(out=d, in_=s),
                       sem=ctr, n=16)

    def wait(self, eng, ctr, val):
        if val is None:
            raise ValueError("wait on None")
        assert val <= ctr.v, f"wait on future value {val} > {ctr.v} ({eng})"
        self.streams[eng].append(lambda e, s=ctr.sem, v=val: e.wait_ge(s, v))

    def run(self, scope=None):
        import contextlib
        ctx = self.nc.named_scope(scope) if scope else contextlib.nullcontext()
        with ctx, self.nc.Block() as block:
            for e in self.ENGS:
                fns = self.streams[e]
                if fns:
                    getattr(block, e)(lambda eng, fns=fns: [f(eng) for f in fns])


def build_nc(kphases=99, p1lvl=4):
    nc = bass.Bass("TRN2", target_bir_lowering=False)

    x_d = nc.dram_tensor("x_all", [NT * 128, D], F32, kind="ExternalInput").ap()
    wqkv_d = nc.dram_tensor("wqkv", [D, 3 * D], F32, kind="ExternalInput").ap()
    wo_d = nc.dram_tensor("wo", [D, D], F32, kind="ExternalInput").ap()
    wfT_d = nc.dram_tensor("wfT", [D, D], F32, kind="ExternalInput").ap()
    rrT_d = nc.dram_tensor("rrT", [D, D], F32, kind="ExternalInput").ap()
    qwb_d = nc.dram_tensor("qwb", [128, D], F32, kind="ExternalInput").ap()
    kwb_d = nc.dram_tensor("kwb", [128, D], F32, kind="ExternalInput").ap()
    cos_d = nc.dram_tensor("cosT", [NT * 128, 64], F32, kind="ExternalInput").ap()
    sin_d = nc.dram_tensor("sinT", [NT * 128, 64], F32, kind="ExternalInput").ap()
    valid_d = nc.dram_tensor("valid", [128, NT], BF16, kind="ExternalInput").ap()
    y_d = nc.dram_tensor("y", [NQ * 128, D], F32, kind="ExternalOutput").ap()
    vhbm = nc.dram_tensor("vhbm", [NT, 128, D], BF16, kind="Internal").ap()

    i_np = np.arange(128)
    m0 = (i_np[None, :] < i_np[:, None]).astype(bfdt)
    m8 = (i_np[None, :] >= i_np[:, None]).astype(bfdt)
    masks_dram = nc.inline_tensor(np.concatenate([m0, m8], axis=1), name="masks_c")
    ident_dram = nc.inline_tensor(np.eye(128, dtype=bfdt), name="ident_c")
    ones_dram = nc.inline_tensor(np.ones((1, 128), dtype=bfdt), name="ones_c")
    pool_np = np.tile(np.array(CONSTS, dtype=np.float32)[None, :], (128, 1))
    pool_dram = nc.inline_tensor(pool_np, name="pool_c")

    sba = lambda n, s, d: nc.alloc_sbuf_tensor(n, s, d).ap()
    R1 = sba("R1", [128, 34816], BF16)
    R2 = sba("R2", [128, 32768], BF16)
    R3 = sba("R3", [128, 16384], BF16)
    wchunk = sba("wchunk", [128, 8192], BF16)
    wstgA = sba("wstgA", [128, 512], F32)
    wstgB = sba("wstgB", [128, 512], F32)
    consts = sba("constpool", [128, len(CONSTS)], F32)
    identb = sba("identb", [128, 128], BF16)
    masks = sba("masksb", [128, 256], BF16)
    ones = sba("onesb", [1, 128], BF16)
    qwb = sba("qwbb", [128, D], BF16)
    kwb = sba("kwbb", [128, D], BF16)
    stats = sba("stats", [128, 6 * NT], F32)
    thr = sba("thr", [128, 2 * NQ], F32)
    ropeo = sba("ropeo", [128, D], BF16)
    sa = sba("sa", [128, 1024], BF16)
    sb_ = sba("sb", [128, 1024], BF16)
    PTb = sba("PTb", [128, 2304], BF16)
    nmaxs = sba("nmaxs", [128, NH], F32)
    nmaxsb = sba("nmaxsb", [128, NH], BF16)
    nmrowA = None
    rinv = sba("rinv", [128, NH], F32)
    validb = sba("validb", [128, NT], BF16)

    for i, cval in enumerate(CONSTS):
        nc.const_aps.register(F32, cval, consts[:, i:i + 1])

    psA = nc.alloc_psum_tensor("psA", [128, 1536], F32).ap()
    psB = nc.alloc_psum_tensor("psB", [128, 1536], F32).ap()
    psC = nc.alloc_psum_tensor("psC", [128, 512], F32).ap()
    psDb = nc.alloc_psum_tensor("psD", [128, 1024], BF16).ap()
    psCb = psC.bitcast(BF16)

    xlnT = R1
    cosb = R1[:, 32768:33792]
    sinb = R1[:, 33792:34816]

    vring = R1
    o_sb_base = NKT * 2048
    kT = R2
    x2 = R2.bitcast(F32)
    qT = R3
    xqT = R3
    xres = R3.bitcast(F32)
    xtile = wchunk.bitcast(F32)
    stg32 = wchunk.bitcast(F32)
    outstg = sa.bitcast(F32)
    nmrow_half = [wstgA.bitcast(BF16), wstgB.bitcast(BF16)]
    rope_x = ropeo
    ro2stgA = ropeo.bitcast(F32)
    sums = lambda t: stats[:, 6 * t + 0:6 * t + 1]
    sumsq = lambda t: stats[:, 6 * t + 1:6 * t + 2]
    msum = lambda t: stats[:, 6 * t + 2:6 * t + 3]
    dvar = lambda t: stats[:, 6 * t + 3:6 * t + 4]
    s_of = lambda t: stats[:, 6 * t + 4:6 * t + 5]
    r_of = lambda t: stats[:, 6 * t + 5:6 * t + 6]

    WS = [Ctr(nc, "wA"), Ctr(nc, "wB"), Ctr(nc, "wC"), Ctr(nc, "wD")]
    dC = Ctr(nc, "dC")
    SP = [Ctr(nc, "spA"), Ctr(nc, "spB")]
    RNG = Ctr(nc, "rng")
    XR = [Ctr(nc, "xr0"), Ctr(nc, "xr1"), Ctr(nc, "xr2"), Ctr(nc, "xr3")]
    dO = Ctr(nc, "dO")
    ct = Ctr(nc, "ct")
    cv = Ctr(nc, "cv")
    cs = Ctr(nc, "cs")
    OWN = {"vector": cv, "scalar": cs}

    class TrPipe:
        def __init__(self):
            self.guard = [None, None]
            self.idx = 0

        def do(self, pl, src_ap, dst_ap, scale=None):
            si = self.idx % 2
            self.idx += 1
            slot = psDb[:, 0:128] if si == 0 else psCb[:, 0:128]
            if self.guard[si] is not None:
                pl.wait("tensor", cs, self.guard[si])
            tv = pl.op("tensor", lambda e, s=slot, x=src_ap: e.transpose(s, x, identb[:]),
                       sem=ct)
            pl.wait("scalar", ct, tv)
            if scale is None:
                g = pl.op("scalar", lambda e, s=slot, dd=dst_ap: e.copy(dd, s),
                          ins=[slot], out=dst_ap, sem=cs)
            else:
                g = pl.op("scalar", lambda e, s=slot, dd=dst_ap, sc=scale: e.activation(
                    out=dd, in_=s, func=AF.Copy, scale=sc), ins=[slot], out=dst_ap, sem=cs)
            self.guard[si] = g
            return g


    p = Plan(nc, OWN)
    p.dma("gpsimd", consts[:], pool_dram.ap(), dC)
    p.dma("gpsimd", identb[:], ident_dram.ap(), dC)
    p.dma("gpsimd", masks[:], masks_dram.ap(), dC)
    p.dma("gpsimd", ones[:], ones_dram.ap(), dC)
    p.dma("gpsimd", validb[:], valid_d, dC)
    dC_final = dC.v
    cos_src = bass.AP(cos_d.tensor, 0, [[64, 128], [128 * 64, 16], [1, 64]])
    sin_src = bass.AP(sin_d.tensor, 0, [[64, 128], [128 * 64, 16], [1, 64]])
    v1 = p.dma("sync", stg32[:, 0:1024], cos_src, WS[0])
    v2 = p.dma("sync", stg32[:, 1024:2048], sin_src, WS[1])
    p.wait("vector", WS[0], v1)
    vc = p.op("vector", lambda e: e.tensor_copy(out=cosb[:], in_=stg32[:, 0:1024]),
              out=cosb[:], sem=cv)
    p.wait("vector", WS[1], v2)
    p.op("vector", lambda e: e.tensor_copy(out=sinb[:], in_=stg32[:, 1024:2048]),
         out=sinb[:])
    p.wait("sync", cv, vc)
    v3 = p.dma("sync", stg32[:, 2048:4096], qwb_d, WS[0])
    p.wait("vector", WS[0], v3)
    vq = p.op("vector", lambda e: e.tensor_copy(out=qwb[:], in_=stg32[:, 2048:4096]),
              out=qwb[:], sem=cv)
    p.wait("sync", cv, vq)
    v4 = p.dma("sync", stg32[:, 2048:4096], kwb_d, WS[0])
    p.wait("vector", WS[0], v4)
    p.op("vector", lambda e: e.tensor_copy(out=kwb[:], in_=stg32[:, 2048:4096]),
         out=kwb[:])
    p.run("P0")

    def _finish_early():
        pf = Plan(nc, OWN)
        for jj in range(NQ):
            for cc in range(4):
                pf.dma("gpsimd", y_d[128 * jj:128 * (jj + 1), 512 * cc:512 * (cc + 1)],
                       x2[:, 2048 * jj + 512 * cc:2048 * jj + 512 * (cc + 1)], dO)
        pf.streams["gpsimd"].append(lambda e, v=dO.v: e.wait_ge(dO.sem, v))
        for cc in [RNG, dC, SP[0], SP[1]] + XR:
            pf.streams["gpsimd"].append(lambda e, s=cc.sem, v=cc.v: e.wait_ge(s, v))
        for cc in WS:
            pf.streams["sync"].append(lambda e, s=cc.sem, v=cc.v: e.wait_ge(s, v))
        pf.run()
        return nc
    if kphases < 1:
        return _finish_early()

    p = Plan(nc, OWN)
    for engx in ("tensor", "scalar", "vector"):
        p.wait(engx, dC, dC_final)
    tr = TrPipe()
    xslot = [xtile[:, 0:2048], xtile[:, 2048:4096]]
    cs_apply = {}
    last_tr_of_tile = {}
    dump = PTb[:, 0:2048]
    for t in range(NT):
        if t >= 2 and p1lvl >= 3:
            p.wait("sync", cs, cs_apply[t - 2])
        ld = p.dma("sync", xslot[t % 2], x_d[128 * t:128 * (t + 1), :], WS[t % 2])
        xs_ = xslot[t % 2]
        if p1lvl < 1:
            continue
        p.wait("scalar", WS[t % 2], ld)
        p.op("scalar", lambda e, x=xs_, a=sums(t): e.activation(
            out=dump, in_=x, func=AF.Copy, accum_out=a), ins=[xs_], out=(dump, sums(t)))
        st1 = p.op("scalar", lambda e, x=xs_, t=t: e.activation(
            out=dump, in_=x, func=AF.Square, accum_out=sumsq(t)),
            ins=[xs_], out=(dump, sumsq(t)), sem=cs)
        if p1lvl < 2:
            continue
        p.wait("vector", cs, st1)
        p.op("vector", lambda e, t=t: e.scalar_tensor_tensor(
            msum(t), sums(t), CPC, sums(t), ALU.mult, ALU.mult),
            ins=[sums(t)], out=msum(t))
        cvd = p.op("vector", lambda e, t=t: e.scalar_tensor_tensor(
            dvar(t), msum(t), -1.0, sumsq(t), ALU.mult, ALU.add),
            ins=[msum(t), sumsq(t)], out=dvar(t), sem=cv)
        p.wait("scalar", cv, cvd)
        css = p.op("scalar", lambda e, t=t: e.activation(
            out=s_of(t), in_=dvar(t), func=AF.Sqrt, scale=CPC, bias=LN_EPS),
            ins=[dvar(t)], out=s_of(t), sem=cs)
        p.wait("vector", cs, css)
        p.op("vector", lambda e, t=t: e.reciprocal(r_of(t), s_of(t)),
             ins=[s_of(t)], out=r_of(t))
        cvr = p.op("vector", lambda e, t=t: e.scalar_tensor_tensor(
            msum(t), sums(t), -CPC, r_of(t), ALU.mult, ALU.mult),
            ins=[sums(t), r_of(t)], out=msum(t), sem=cv)
        if p1lvl < 3:
            continue
        p.wait("scalar", cv, cvr)
        if t >= 1:
            p.wait("scalar", ct, last_tr_of_tile[t - 1])
        cs_apply[t] = p.op("scalar", lambda e, t=t, x=xs_: e.activation(
            out=ropeo[:], in_=x, func=AF.Identity, scale=r_of(t), bias=msum(t)),
            ins=[xs_, r_of(t), msum(t)], out=ropeo[:], sem=cs)
        if p1lvl < 4:
            last_tr_of_tile[t] = ct.v
            continue
        p.wait("tensor", cs, cs_apply[t])
        for d in range(16):
            dst = xlnT[:, 128 * (16 * t + d):128 * (16 * t + d + 1)]
            tr.do(p, ropeo[:, 128 * d:128 * (d + 1)], dst)
        last_tr_of_tile[t] = ct.v
    p.run("P1")
    if kphases < 2:
        return _finish_early()

    def qkv_pass(pl, col0, nchunks, tiles, dst_fn, pre_evac=None, post_evac=None,
                 w_dram=None, lhs_col=None):
        if w_dram is None:
            w_dram = wqkv_d
        if lhs_col is None:
            lhs_col = lambda t, d: 128 * (16 * t + d)
        ev = {}
        stg_guard = [None, None]
        par_guard = [None, None]
        prev_mm = None
        for ci in range(nchunks):
            c0 = col0 + 512 * ci
            if prev_mm is not None:
                pl.wait("vector", ct, prev_mm)
            for d in range(16):
                si = d % 2
                stg = wstgA if si == 0 else wstgB
                if stg_guard[si] is not None:
                    pl.wait("sync", cv, stg_guard[si])
                ld = pl.dma("sync", stg[:], w_dram[128 * d:128 * (d + 1), c0:c0 + 512],
                            WS[si])
                pl.wait("vector", WS[si], ld)
                dstw = wchunk[:, 512 * d:512 * (d + 1)]
                stg_guard[si] = pl.op("vector", lambda e, s=stg, w=dstw: e.tensor_copy(
                    out=w, in_=s), ins=[stg[:]], out=dstw, sem=cv)
            cast15 = stg_guard[1]
            pl.wait("tensor", cv, cast15)
            for ti, t in enumerate(tiles):
                par = ti % 2
                ps = (psA if par == 0 else psB)[:, 0:512]
                if par_guard[par] is not None:
                    pl.wait("tensor", cs, par_guard[par])
                ctm = None
                for d in range(16):
                    ctm = pl.op("tensor", lambda e, d=d, t=t, ps=ps: e.matmul(
                        ps, xlnT[:, lhs_col(t, d):lhs_col(t, d) + 128],
                        wchunk[:, 512 * d:512 * (d + 1)],
                        start=(d == 0), stop=(d == 15)),
                        sem=(ct if d == 15 else None))
                pl.wait("scalar", ct, ctm)
                if pre_evac is not None:
                    pre_evac(pl, t, ci)
                dst = dst_fn(t, ci)
                ev[(t, ci)] = pl.op("scalar", lambda e, ps=ps, dd=dst: e.copy(dd, ps),
                                    ins=[ps], out=dst, sem=cs)
                par_guard[par] = ev[(t, ci)]
                if post_evac is not None:
                    post_evac(pl, t, ci, ev[(t, ci)])
            prev_mm = ctm
        return ev

    def postprocess(pl, region, tiles, blkstride, base_of, w_b, sqscale, sqbias,
                    ev, last_ci, tq_of):
        rowstride = region.tensor.shape[1]

        def rows(t):
            return bass.AP(region.tensor, 128 * base_of(t),
                           [[rowstride, 128], [blkstride, 16], [1, 128]])

        def half(t, hh):
            return bass.AP(region.tensor, 128 * base_of(t) + 64 * hh,
                           [[rowstride, 128], [blkstride, 16], [1, 64]])

        for t in tiles:
            pl.wait("scalar", cs, ev[(t, last_ci)])
            pl.op("scalar", lambda e, s=rows(t), t=t: e.activation(
                out=dump, in_=s, func=AF.Square, accum_out=sumsq(t)),
                ins=[rows(t)], out=(dump, sumsq(t)))
        for t in tiles:
            pl.op("scalar", lambda e, t=t: e.activation(
                out=s_of(t), in_=sumsq(t), func=AF.Sqrt, scale=sqscale, bias=sqbias),
                ins=[sumsq(t)], out=s_of(t), sem=cs)
        cs_sqrt = cs.v
        pl.wait("vector", cs, cs_sqrt)
        for t in tiles:
            pl.op("vector", lambda e, t=t: e.reciprocal(r_of(t), s_of(t)),
                  ins=[s_of(t)], out=r_of(t), sem=cv)
        cv_r = cv.v
        pl.wait("scalar", cv, cv_r)
        for t in tiles:
            pl.op("scalar", lambda e, s=rows(t), t=t: e.activation(
                out=s, in_=s, func=AF.Copy, scale=r_of(t)),
                ins=[rows(t), r_of(t)], out=rows(t), sem=cs)
        cs_ap = cs.v
        pl.wait("vector", cs, cs_ap)
        wap = bass.AP(w_b.tensor, 0, [[2048, 128], [128, 16], [1, 128]])
        for t in tiles:
            pl.op("vector", lambda e, s=rows(t): e.tensor_mul(s, s, wap),
                  ins=[rows(t)], out=rows(t), sem=cv)
        cv_w = cv.v
        pl.wait("vector", cv, cv_w)
        tr2 = TrPipe()
        prev_tr = None
        for t in tiles:
            x1 = half(t, 0)
            x2_ = half(t, 1)
            ro1 = bass.AP(ropeo.tensor, 0, [[2048, 128], [128, 16], [1, 64]])
            ro2 = bass.AP(ropeo.tensor, 64, [[2048, 128], [128, 16], [1, 64]])
            sa_a = bass.AP(sa.tensor, 0, [[1024, 128], [64, 16], [1, 64]])
            sb_a = bass.AP(sb_.tensor, 0, [[1024, 128], [64, 16], [1, 64]])
            c64 = cosb[:, 64 * t:64 * t + 64]
            s64 = sinb[:, 64 * t:64 * t + 64]
            cos_a = bass.AP(c64.tensor, c64.offset,
                            [list(c64.ap[0]), [0, 16], list(c64.ap[1])])
            sin_a = bass.AP(s64.tensor, s64.offset,
                            [list(s64.ap[0]), [0, 16], list(s64.ap[1])])
            if prev_tr is not None:
                pl.wait("vector", ct, prev_tr)
            pl.op("vector", lambda e, a=ro1, b=x1, c=cos_a: e.tensor_mul(a, b, c),
                  ins=[x1], out=ro1)
            pl.op("vector", lambda e, a=sa_a, b=x2_, c=sin_a: e.tensor_mul(a, b, c),
                  ins=[x2_], out=sa_a)
            pl.op("vector", lambda e, a=ro2, b=x2_, c=cos_a: e.tensor_mul(a, b, c),
                  ins=[x2_], out=ro2)
            pl.op("vector", lambda e, a=sb_a, b=x1, c=sin_a: e.tensor_mul(a, b, c),
                  ins=[x1], out=sb_a)
            pl.op("vector", lambda e, a=ro1, b=sa_a: e.tensor_sub(a, a, b),
                  ins=[ro1, sa_a], out=ro1)
            cvrope = pl.op("vector", lambda e, a=ro2, b=sb_a: e.tensor_add(a, a, b),
                           ins=[ro2, sb_a], out=ro2, sem=cv)
            pl.wait("tensor", cv, cvrope)
            for h in range(16):
                dst = bass.AP(region.tensor, blkstride * h + 128 * tq_of(t),
                              [[rowstride, 128], [1, 128]])
                tr2.do(pl, ropeo[:, 128 * h:128 * (h + 1)], dst)
            prev_tr = ct.v

    p = Plan(nc, OWN)

    def k_dst(t, ci):
        return bass.AP(kT.tensor, 2048 * (4 * ci) + 128 * t,
                       [[32768, 128], [2048, 4], [1, 128]])

    ev_k = qkv_pass(p, 2048, 4, list(range(NT)), k_dst)
    postprocess(p, kT, list(range(NT)), 2048, lambda t: t, kwb, CPC, RMS_EPS,
                ev_k, 3, tq_of=lambda t: t)
    p.run("P2k")
    if kphases < 3:
        return _finish_early()

    p = Plan(nc, OWN)

    def q_dst(t, ci):
        return bass.AP(qT.tensor, 1024 * (4 * ci) + 128 * (t - 8),
                       [[16384, 128], [1024, 4], [1, 128]])

    own = list(range(8, 16))
    ev_q = qkv_pass(p, 0, 4, own, q_dst)
    postprocess(p, qT, own, 1024, lambda t: t - 8, qwb, 1.0 / 16.0, 128.0 * RMS_EPS,
                ev_q, 3, tq_of=lambda t: t - 8)
    p.run("P3q")
    if kphases < 4:
        return _finish_early()

    p = Plan(nc, OWN)
    vstg = [PTb[:, 0:512], PTb[:, 512:1024]]
    spill_guard = [None, None]

    def v_dst(t, ci):
        return vstg[(16 * ci + t) % 2][:]

    def v_pre(pl, t, ci):
        si = (16 * ci + t) % 2
        if spill_guard[si] is not None:
            pl.wait("scalar", SP[si], spill_guard[si])

    def v_post(pl, t, ci, evv):
        si = (16 * ci + t) % 2
        pl.wait("gpsimd", cs, evv)
        spill_guard[si] = pl.dma(
            "gpsimd", bass.AP(vhbm.tensor, t * 128 * D + 512 * ci, [[D, 128], [1, 512]]),
            vstg[si][:], SP[si])

    qkv_pass(p, 4096, 4, list(range(NT)), v_dst, pre_evac=v_pre, post_evac=v_post)
    p.run("P4v")
    if kphases < 5:
        return _finish_early()

    p = Plan(nc, OWN)
    p.wait("gpsimd", SP[0], SP[0].v)
    p.wait("gpsimd", SP[1], SP[1].v)
    ring_ld = {}
    for l in range(NKT):
        ring_ld[l] = p.dma("gpsimd",
                           vring[:, 2048 * (l % NKT):2048 * (l % NKT) + 2048],
                           bass.AP(vhbm.tensor, l * 128 * D, [[D, 128], [1, 2048]]), RNG)
    bank_guard = [None, None]
    pt_guard = [None, None]
    psc_guard = None
    ring_guard = {}
    o_col = lambda j, h: o_sb_base + 2048 * j + 128 * h
    for j in range(NQ):
        if j >= 1:
            sl = (j + 8) % NKT
            p.wait("gpsimd", ct, ring_guard[j - 1])
            ring_ld[j + 8] = p.dma(
                "gpsimd", vring[:, 2048 * sl:2048 * sl + 2048],
                bass.AP(vhbm.tensor, (j + 8) * 128 * D, [[D, 128], [1, 2048]]), RNG)
        red = {}
        for h in range(NH):
            par = h % 2
            ps = (psA if par == 0 else psB)
            if bank_guard[par] is not None:
                c, v = bank_guard[par]
                p.wait("tensor", c, v)
            q_ap = qT[:, 1024 * h + 128 * j:1024 * h + 128 * (j + 1)]
            ctm = None
            for (off, n) in ((0, 512), (512, 512), (1024, 128)):
                kbase = 2048 * h + 128 * j
                ctm = p.op("tensor", lambda e, ps=ps, q=q_ap, kb=kbase, o=off, n=n: e.matmul(
                    ps[:, o:o + n], q, kT[:, kb + o:kb + o + n], start=True, stop=True),
                    sem=(ct if off == 1024 else None))
            p.wait("vector", ct, ctm)
            red[h] = p.op("vector", lambda e, ps=ps, h=h: e.tensor_reduce(
                out=nmaxs[:, h:h + 1], in_=ps[:, 0:1152], axis=AX.X, op=ALU.max,
                negate=True), ins=[ps[:, 0:1152]], out=nmaxs[:, h:h + 1], sem=cv)
            bank_guard[par] = (cv, red[h])
        p.op("vector", lambda e: e.tensor_copy(out=nmaxsb[:, 0:8], in_=nmaxs[:, 0:8]),
             ins=[nmaxs[:, 0:8]], out=nmaxsb[:, 0:8])
        castb = p.op("vector", lambda e: e.tensor_copy(out=nmaxsb[:, 8:16],
                                                       in_=nmaxs[:, 8:16]),
                     ins=[nmaxs[:, 8:16]], out=nmaxsb[:, 8:16], sem=cv)
        p.wait("tensor", cv, castb)
        nm = {}
        for b in range(2):
            if b == 1:
                p.wait("tensor", cs, nm[0])
            elif j >= 1:
                p.wait("tensor", cs, nm2_prev)
            tv = None
            for h in range(8 * b, 8 * b + 8):
                tv = p.op("tensor", lambda e, h=h, b=b: e.transpose(
                    psDb[0:1, 128 * (h - 8 * b):128 * (h - 8 * b + 1)],
                    nmaxsb[:, h:h + 1], identb[:]), sem=(ct if h == 8 * b + 7 else None))
            p.wait("scalar", ct, tv)
            nm[b] = p.op("scalar", lambda e, b=b: e.copy(
                nmrow_half[b][0:1, 0:1024], psDb[0:1, 0:1024]),
                ins=[psDb[0:1, 0:1024]], out=nmrow_half[b][0:1, 0:1024], sem=cs)
        nm2 = nm[1]
        nm2_prev = nm2
        p.wait("tensor", RNG, ring_ld[j + 8])
        mk_of = {}
        exp_of = {}

        def emit_masks(h):
            PT = PTb[:, 1152 * (h % 2):1152 * (h % 2 + 1)]
            p.wait("vector", cs, exp_of[h])
            p.op("vector", lambda e, PT=PT: e.tensor_mul(
                PT[:, 0:128], PT[:, 0:128], masks[:, 0:128]),
                ins=[PT[:, 0:128]], out=PT[:, 0:128])
            mk_of[h] = p.op("vector", lambda e, PT=PT: e.tensor_mul(
                PT[:, 1024:1152], PT[:, 1024:1152], masks[:, 128:256]),
                ins=[PT[:, 1024:1152]], out=PT[:, 1024:1152], sem=cv)

        def emit_ST(h):
            par = h % 2
            ps = psA if par == 0 else psB
            if bank_guard[par] is not None:
                c, v = bank_guard[par]
                p.wait("tensor", c, v)
            if h == 0:
                p.wait("tensor", cs, nm2)
            kbase = 2048 * h + 128 * j
            q_ap = qT[:, 1024 * h + 128 * j:1024 * h + 128 * (j + 1)]
            ctst = None
            for l in range(NKT):
                p.op("tensor", lambda e, ps=ps, l=l, kb=kbase, q=q_ap: e.matmul(
                    ps[:, 128 * l:128 * (l + 1)],
                    kT[:, kb + 128 * l:kb + 128 * (l + 1)],
                    q, start=True, stop=False))
                nmr = nmrow_half[h // 8][0:1, 128 * (h % 8):128 * (h % 8 + 1)]
                ctst = p.op("tensor", lambda e, ps=ps, l=l, r=nmr: e.matmul(
                    ps[:, 128 * l:128 * (l + 1)], ones[:], r, start=False, stop=True),
                    sem=(ct if l == NKT - 1 else None))
            PT = PTb[:, 1152 * par:1152 * (par + 1)]
            p.wait("scalar", ct, ctst)
            if pt_guard[par] is not None:
                c, v = pt_guard[par]
                p.wait("scalar", c, v)
            expv = p.op("scalar", lambda e, ps=ps, PT=PT: e.activation(
                out=PT, in_=ps[:, 0:1152], func=AF.Exp), ins=[ps[:, 0:1152]], out=PT,
                sem=cs)
            bank_guard[par] = (cs, expv)
            exp_of[h] = expv

        def emit_PV(h):
            nonlocal psc_guard
            par = h % 2
            PT = PTb[:, 1152 * par:1152 * (par + 1)]
            p.wait("tensor", cv, mk_of[h])
            if psc_guard is not None:
                p.wait("tensor", cv, psc_guard)
            for l in range(NKT):
                sl = (j + l) % NKT
                p.op("tensor", lambda e, PT=PT, l=l, sl=sl, h=h: e.matmul(
                    psC[:, 0:128], PT[:, 128 * l:128 * (l + 1)],
                    vring[:, 2048 * sl + 128 * h:2048 * sl + 128 * (h + 1)],
                    start=(l == 0), stop=(l == NKT - 1)))
            ctpv = None
            for l in range(NKT):
                ctpv = p.op("tensor", lambda e, PT=PT, l=l, j=j: e.matmul(
                    psC[:, 128:129], PT[:, 128 * l:128 * (l + 1)],
                    validb[:, j + l:j + l + 1], start=(l == 0), stop=(l == NKT - 1)),
                    sem=(ct if l == NKT - 1 else None))
            pt_guard[par] = (ct, ctpv)
            p.wait("vector", ct, ctpv)
            p.op("vector", lambda e, h=h: e.reciprocal(
                rinv[:, h:h + 1], psC[:, 128:129]), ins=[psC[:, 128:129]],
                out=rinv[:, h:h + 1])
            if h + 1 < NH:
                emit_masks(h + 1)
            oc = o_col(j, h)
            psc_guard = p.op("vector", lambda e, h=h, oc=oc: e.tensor_scalar_mul(
                R1[:, oc:oc + 128], psC[:, 0:128], rinv[:, h:h + 1]),
                ins=[psC[:, 0:128], rinv[:, h:h + 1]],
                out=R1[:, oc:oc + 128], sem=cv)

        emit_ST(0)
        emit_masks(0)
        for h in range(1, NH):
            emit_ST(h)
            emit_PV(h - 1)
        emit_PV(NH - 1)
        ring_guard[j] = ct.v
    p.run("P5attn")
    if kphases < 6:
        return _finish_early()

    p = Plan(nc, OWN)
    tr6 = TrPipe()
    for j in range(NQ):
        for h in range(NH):
            c0 = o_sb_base + 2048 * j + 128 * h
            tr6.do(p, R1[:, c0:c0 + 128], R1[:, c0:c0 + 128])
    cs_oT = cs.v
    p.wait("tensor", cs, cs_oT)
    xres_guard = [None, None, None, None]
    stg_guard6 = [None, None]
    par_guard6 = [None, None]
    prev_mm6 = None
    for ci in range(4):
        c0 = 512 * ci
        if prev_mm6 is not None:
            p.wait("vector", ct, prev_mm6)
        for d in range(16):
            si = d % 2
            stg = wstgA if si == 0 else wstgB
            if stg_guard6[si] is not None:
                p.wait("sync", cv, stg_guard6[si])
            ld = p.dma("sync", stg[:], wo_d[128 * d:128 * (d + 1), c0:c0 + 512], WS[si])
            p.wait("vector", WS[si], ld)
            dstw = wchunk[:, 512 * d:512 * (d + 1)]
            stg_guard6[si] = p.op("vector", lambda e, s=stg, w=dstw: e.tensor_copy(
                out=w, in_=s), ins=[stg[:]], out=dstw, sem=cv)
        p.wait("tensor", cv, stg_guard6[1])
        for j in range(NQ):
            si = j % 4
            ps = (psA if j % 2 == 0 else psB)[:, 0:512]
            if xres_guard[si] is not None:
                p.wait("gpsimd", cv, xres_guard[si])
            xl = p.dma("gpsimd", xres[:, 512 * si:512 * (si + 1)],
                       x_d[128 * (8 + j):128 * (9 + j), c0:c0 + 512], XR[si])
            if par_guard6[j % 2] is not None:
                p.wait("tensor", cv, par_guard6[j % 2])
            ctm = None
            for h in range(NH):
                c0o = o_sb_base + 2048 * j + 128 * h
                ctm = p.op("tensor", lambda e, ps=ps, c0o=c0o, h=h: e.matmul(
                    ps, R1[:, c0o:c0o + 128], wchunk[:, 512 * h:512 * (h + 1)],
                    start=(h == 0), stop=(h == NH - 1)),
                    sem=(ct if h == NH - 1 else None))
            p.wait("vector", ct, ctm)
            p.wait("vector", XR[si], xl)
            dstx = x2[:, 2048 * j + c0:2048 * j + c0 + 512]
            evj = p.op("vector", lambda e, ps=ps, si=si, dd=dstx: e.tensor_add(
                dd, ps, xres[:, 512 * si:512 * (si + 1)]),
                ins=[ps, xres[:, 512 * si:512 * (si + 1)]], out=dstx, sem=cv)
            xres_guard[si] = evj
            par_guard6[j % 2] = evj
        prev_mm6 = ctm
    p.run("P6wo")
    if kphases < 7:
        return _finish_early()

    p = Plan(nc, OWN)
    th_of = lambda j: thr[:, 2 * j:2 * j + 1]
    tl_of = lambda j: thr[:, 2 * j + 1:2 * j + 2]
    tr7 = TrPipe()
    cvx_prev = None
    for j in range(NQ):
        xt = x2[:, 2048 * j:2048 * (j + 1)]
        if cvx_prev is not None:
            p.wait("scalar", cv, cvx_prev)
        p.op("scalar", lambda e, x=xt, a=sums(j): e.activation(
            out=dump, in_=x, func=AF.Copy, accum_out=a), ins=[xt], out=(dump, sums(j)))
        stv = p.op("scalar", lambda e, x=xt, a=sumsq(j): e.activation(
            out=dump, in_=x, func=AF.Square, accum_out=a), ins=[xt],
            out=(dump, sumsq(j)), sem=cs)
        p.wait("vector", cs, stv)
        p.op("vector", lambda e, j=j: e.scalar_tensor_tensor(
            msum(j), sums(j), CPC, sums(j), ALU.mult, ALU.mult),
            ins=[sums(j)], out=msum(j))
        cvd = p.op("vector", lambda e, j=j: e.scalar_tensor_tensor(
            dvar(j), msum(j), -1.0, sumsq(j), ALU.mult, ALU.add),
            ins=[msum(j), sumsq(j)], out=dvar(j), sem=cv)
        p.wait("scalar", cv, cvd)
        csq = p.op("scalar", lambda e, j=j: e.activation(
            out=s_of(j), in_=dvar(j), func=AF.Sqrt, scale=CPC, bias=LN_EPS),
            ins=[dvar(j)], out=s_of(j), sem=cs)
        p.wait("vector", cs, csq)
        p.op("vector", lambda e, j=j: e.tensor_scalar_mul(r_of(j), sums(j), CPC),
             ins=[sums(j)], out=r_of(j))
        p.op("vector", lambda e, j=j: e.scalar_tensor_tensor(
            th_of(j), s_of(j), 0.33, r_of(j), ALU.mult, ALU.add),
            ins=[s_of(j), r_of(j)], out=th_of(j))
        cvt = p.op("vector", lambda e, j=j: e.scalar_tensor_tensor(
            tl_of(j), s_of(j), -0.33, r_of(j), ALU.mult, ALU.add),
            ins=[s_of(j), r_of(j)], out=tl_of(j), sem=cv)
        p.wait("vector", cv, cvt)
        g_d = PTb[:, 0:2048]
        l_d = ropeo[:, 0:2048]
        p.op("vector", lambda e, x=xt, j=j: e.tensor_single_scalar(
            g_d, x, th_of(j), ALU.is_gt), ins=[xt, th_of(j)], out=g_d)
        p.op("vector", lambda e, x=xt, j=j: e.tensor_single_scalar(
            l_d, x, tl_of(j), ALU.is_lt), ins=[xt, tl_of(j)], out=l_d)
        xq_t = xqT[:, 2048 * j:2048 * (j + 1)]
        cvx = p.op("vector", lambda e, a=xq_t, g=g_d, l=l_d: e.tensor_sub(a, g, l),
                   ins=[g_d, l_d], out=xq_t, sem=cv)
        cvx_prev = cvx
        p.wait("tensor", cv, cvx)
        for d in range(16):
            c0 = 2048 * j + 128 * d
            tr7.do(p, xqT[:, c0:c0 + 128], xqT[:, c0:c0 + 128], scale=0.5)
    cs_xqT = cs.v
    p.run("P7ln2")
    if kphases < 8:
        return _finish_early()

    p = Plan(nc, OWN)
    tmpA = [PTb[:, 0:512], PTb[:, 1024:1536]]
    tmpB = [PTb[:, 512:1024], PTb[:, 1536:2048]]
    stgW = [wstgA, ro2stgA[:, 0:512]]
    stgR = [wstgB, ro2stgA[:, 512:1024]]
    out_guard = None
    built_guard = [None, None]
    par_guard8 = [None, None]
    prev_mm8 = None
    for ci in range(4):
        c0 = 512 * ci
        if prev_mm8 is not None:
            p.wait("vector", ct, prev_mm8)
        for d in range(16):
            si = d % 2
            if built_guard[si] is not None:
                p.wait("sync", cv, built_guard[si])
                p.wait("scalar", cv, built_guard[si])
            ldw = p.dma("sync", stgW[si][:], wfT_d[128 * d:128 * (d + 1), c0:c0 + 512],
                        WS[si])
            ldr = p.dma("sync", stgR[si][:], rrT_d[128 * d:128 * (d + 1), c0:c0 + 512],
                        WS[2 + si])
            p.wait("scalar", WS[si], ldw)
            p.op("scalar", lambda e, s=stgW[si], tA=tmpA[si]: e.activation(
                out=tA, in_=s[:], func=AF.Sign, bias=-0.2), ins=[stgW[si][:]], out=tmpA[si])
            sgn = p.op("scalar", lambda e, s=stgW[si], tB=tmpB[si]: e.activation(
                out=tB, in_=s[:], func=AF.Sign, bias=0.2), ins=[stgW[si][:]], out=tmpB[si],
                sem=cs)
            p.wait("vector", cs, sgn)
            p.op("vector", lambda e, tA=tmpA[si], tB=tmpB[si]: e.tensor_add(tA, tA, tB),
                 ins=[tmpA[si], tmpB[si]], out=tmpA[si])
            p.wait("vector", WS[2 + si], ldr)
            dstw = wchunk[:, 512 * d:512 * (d + 1)]
            built_guard[si] = p.op(
                "vector", lambda e, s=stgR[si], w=dstw, tA=tmpA[si]: e.scalar_tensor_tensor(
                    w, s[:], 3.3333333, tA, ALU.mult, ALU.add),
                ins=[stgR[si][:], tmpA[si]], out=dstw, sem=cv)
        p.wait("tensor", cv, built_guard[1])
        if ci == 0:
            p.wait("tensor", cs, cs_xqT)
        for j in range(NQ):
            ps = (psA if j % 2 == 0 else psB)[:, 0:512]
            if par_guard8[j % 2] is not None:
                p.wait("tensor", cv, par_guard8[j % 2])
            ctm = None
            for d in range(16):
                c0x = 2048 * j + 128 * d
                ctm = p.op("tensor", lambda e, ps=ps, c0x=c0x, d=d: e.matmul(
                    ps, xqT[:, c0x:c0x + 128], wchunk[:, 512 * d:512 * (d + 1)],
                    start=(d == 0), stop=(d == 15)),
                    sem=(ct if d == 15 else None))
            p.wait("vector", ct, ctm)
            if out_guard is not None:
                p.wait("vector", dO, out_guard)
            dstx = x2[:, 2048 * j + c0:2048 * j + c0 + 512]
            evo = p.op("vector", lambda e, ps=ps, xx=dstx: e.scalar_tensor_tensor(
                outstg[:, 0:512], ps, 0.3, xx, ALU.mult, ALU.add),
                ins=[ps, dstx], out=outstg[:, 0:512], sem=cv)
            par_guard8[j % 2] = evo
            p.wait("gpsimd", cv, evo)
            out_guard = p.dma("gpsimd", y_d[128 * j:128 * (j + 1), c0:c0 + 512],
                              outstg[:, 0:512], dO)
        prev_mm8 = ctm
    for cc in [dO, RNG, dC, SP[0], SP[1]] + XR:
        p.streams["gpsimd"].append(lambda e, s=cc.sem, v=cc.v: e.wait_ge(s, v))
    for cc in WS:
        p.streams["sync"].append(lambda e, s=cc.sem, v=cc.v: e.wait_ge(s, v))
    p.run("P8ffn")

    return nc


_NC_CACHE = {}


def kernel(L0_attn_hidden_states, L0_attn_Wqkv, L0_attn_q_norm_w, L0_attn_k_norm_w,
           L0_attn_Wo, L0_ffn_weight, L0_ffn_rram_variation):
    X = np.ascontiguousarray(np.asarray(L0_attn_hidden_states, dtype=np.float32)
                             .reshape(S, D))
    wqkv = np.ascontiguousarray(np.asarray(L0_attn_Wqkv, dtype=np.float32))
    wo = np.ascontiguousarray(np.asarray(L0_attn_Wo, dtype=np.float32))
    qw = np.asarray(L0_attn_q_norm_w, dtype=np.float32)
    kw = np.asarray(L0_attn_k_norm_w, dtype=np.float32)
    wfT = np.ascontiguousarray(np.asarray(L0_ffn_weight, dtype=np.float32).T)
    rrT = np.ascontiguousarray(np.asarray(L0_ffn_rram_variation, dtype=np.float32).T)

    qwb = np.ascontiguousarray(np.broadcast_to(qw[None, :], (128, D)))
    kwb = np.ascontiguousarray(np.broadcast_to(kw[None, :], (128, D)))

    inv_freq = 1.0 / THETA ** (np.arange(0, 64, dtype=np.float64) / 64.0)

    if "nc" not in _NC_CACHE:
        _NC_CACHE["nc"] = build_nc()
    nc = _NC_CACHE["nc"]

    in_maps = []
    for c in range(NCORES):
        t0 = 1024 * c - 1024
        if c == 0:
            x_all = np.concatenate([np.zeros((1024, D), np.float32),
                                    X[0:1024]], axis=0)
        else:
            x_all = X[t0:t0 + 2048]
        pos = np.arange(t0, t0 + 2048, dtype=np.float64)
        pos = np.clip(pos, 0, None)
        fr = pos[:, None] * inv_freq[None, :]
        cosT = np.cos(fr).astype(np.float32)
        sinT = np.sin(fr).astype(np.float32)
        valid = np.ones((128, NT), dtype=bfdt)
        if c == 0:
            valid[:, 0:8] = 0
        in_maps.append({
            "x_all": np.ascontiguousarray(x_all),
            "wqkv": wqkv, "wo": wo, "wfT": wfT, "rrT": rrT,
            "qwb": qwb, "kwb": kwb,
            "cosT": cosT, "sinT": sinT, "valid": valid,
        })

    res = run_bass_kernel_spmd(nc, in_maps, core_ids=list(range(NCORES)))
    out = np.concatenate([res.results[c]["y"] for c in range(NCORES)], axis=0)
    return out.reshape(1, S, D)


# version 13 (final)
# speedup vs baseline: 1.0623x; 1.0142x over previous
import numpy as np
import ml_dtypes
import concourse.bass as bass
import concourse.mybir as mybir
from concourse.bass_utils import run_bass_kernel_spmd

F32 = mybir.dt.float32
BF16 = mybir.dt.bfloat16
AF = mybir.ActivationFunctionType
ALU = mybir.AluOpType
AX = mybir.AxisListType
bfdt = ml_dtypes.bfloat16

NCORES = 8
S, D = 8192, 2048
NH = 16
NT = 16
NQ = 8
NKT = 9
LN_EPS = 1e-5
RMS_EPS = 1e-6
THETA = 10000.0
CPC = 1.0 / 2048.0

CONSTS = [CPC, LN_EPS, RMS_EPS, 1.0 / 16.0, 128.0 * RMS_EPS, -0.2, 0.2,
          0.5, 1.0, 0.0, 0.33, -0.33]


class Ctr:
    def __init__(self, nc, name):
        self.sem = nc.alloc_semaphore(name)
        self.v = 0


class Plan:
    ENGS = ("sync", "gpsimd", "vector", "scalar", "tensor")

    def __init__(self, nc, ctrs=None):
        self.nc = nc
        self.streams = {e: [] for e in self.ENGS}
        self.last_out = {"vector": None, "scalar": None}
        self.wrote = {"vector": {}, "scalar": {}}
        self.selfwait = {"vector": 0, "scalar": 0}
        self.own = ctrs or {}

    @staticmethod
    def _rng(ap):
        if ap is None:
            return None
        try:
            import concourse.mybir as _mb
            off = ap.offset if isinstance(ap.offset, int) else -1
            esz = _mb.dt.size(ap.dtype)
            apl = list(ap.ap)
            simple = len(apl) == 2 and apl[-1][0] == 1
            if off < 0 or not simple:
                return (ap.tensor.name, -1, 0)
            return (ap.tensor.name, off * esz, apl[-1][1] * esz)
        except Exception:
            return (str(ap), -1, 0)

    def _hazard(self, eng, in_aps):
        lo = self.last_out.get(eng)
        if lo is None:
            return False
        ln, loff, lsz = lo
        for ap in in_aps:
            r = self._rng(ap)
            if r is None or r[0] != ln:
                continue
            off, sz = r[1], r[2]
            if loff < 0 or off < 0:
                return True
            if off < loff + lsz and loff < off + sz:
                return True
        return False

    def op(self, eng, fn, ins=(), out=None, sem=None, n=1):
        outs = list(out) if isinstance(out, (list, tuple)) else (
            [out] if out is not None else [])
        drains = eng in ("vector", "scalar") and self._hazard(eng, ins)
        if eng in ("vector", "scalar"):
            self.last_out[eng] = self._rng(outs[0]) if outs else None
            own = self.own[eng]
            if sem is None:
                sem, n = own, 1
            need = 0
            names = set()
            for ap in list(ins) + outs:
                r = self._rng(ap)
                if r is not None:
                    names.add(r[0])
            for nm in names:
                need = max(need, self.wrote[eng].get(nm, 0))
            if need > self.selfwait[eng]:
                self.wait(eng, own, need)
                self.selfwait[eng] = need
        if sem is not None:
            sem.v += n
        tgt = sem.v if sem is not None else None
        semh = sem.sem if sem is not None else None
        if eng in ("vector", "scalar") and outs and sem is self.own[eng]:
            for o in outs:
                r = self._rng(o)
                if r is not None:
                    self.wrote[eng][r[0]] = tgt

        def go(e, fn=fn, drains=drains, semh=semh, n=n):
            if drains:
                e.drain()
            ins_ = fn(e)
            if semh is not None:
                ins_.then_inc(semh, n)
        self.streams[eng].append(go)
        return tgt

    def dma(self, eng, dst, src, ctr):
        return self.op(eng, lambda e, d=dst, s=src: e.dma_start(out=d, in_=s),
                       sem=ctr, n=16)

    def wait(self, eng, ctr, val):
        if val is None:
            raise ValueError("wait on None")
        assert val <= ctr.v, f"wait on future value {val} > {ctr.v} ({eng})"
        self.streams[eng].append(lambda e, s=ctr.sem, v=val: e.wait_ge(s, v))

    def run(self, scope=None):
        import contextlib
        ctx = self.nc.named_scope(scope) if scope else contextlib.nullcontext()
        with ctx, self.nc.Block() as block:
            for e in self.ENGS:
                fns = self.streams[e]
                if fns:
                    getattr(block, e)(lambda eng, fns=fns: [f(eng) for f in fns])


def build_nc(kphases=99, p1lvl=4):
    nc = bass.Bass("TRN2", target_bir_lowering=False)

    x_d = nc.dram_tensor("x_all", [NT * 128, D], F32, kind="ExternalInput").ap()
    wqkv_d = nc.dram_tensor("wqkv", [D, 3 * D], F32, kind="ExternalInput").ap()
    wo_d = nc.dram_tensor("wo", [D, D], F32, kind="ExternalInput").ap()
    wfT_d = nc.dram_tensor("wfT", [D, D], F32, kind="ExternalInput").ap()
    rrT_d = nc.dram_tensor("rrT", [D, D], F32, kind="ExternalInput").ap()
    qwb_d = nc.dram_tensor("qwb", [128, D], F32, kind="ExternalInput").ap()
    kwb_d = nc.dram_tensor("kwb", [128, D], F32, kind="ExternalInput").ap()
    cos_d = nc.dram_tensor("cosT", [NT * 128, 64], F32, kind="ExternalInput").ap()
    sin_d = nc.dram_tensor("sinT", [NT * 128, 64], F32, kind="ExternalInput").ap()
    valid_d = nc.dram_tensor("valid", [128, NT], BF16, kind="ExternalInput").ap()
    y_d = nc.dram_tensor("y", [NQ * 128, D], F32, kind="ExternalOutput").ap()
    vhbm = nc.dram_tensor("vhbm", [NT, 128, D], BF16, kind="Internal").ap()

    i_np = np.arange(128)
    m0 = (i_np[None, :] < i_np[:, None]).astype(bfdt)
    m8 = (i_np[None, :] >= i_np[:, None]).astype(bfdt)
    masks_dram = nc.inline_tensor(np.concatenate([m0, m8], axis=1), name="masks_c")
    ident_dram = nc.inline_tensor(np.eye(128, dtype=bfdt), name="ident_c")
    ones_dram = nc.inline_tensor(np.ones((1, 128), dtype=bfdt), name="ones_c")
    pool_np = np.tile(np.array(CONSTS, dtype=np.float32)[None, :], (128, 1))
    pool_dram = nc.inline_tensor(pool_np, name="pool_c")

    sba = lambda n, s, d: nc.alloc_sbuf_tensor(n, s, d).ap()
    R1 = sba("R1", [128, 34816], BF16)
    R2 = sba("R2", [128, 32768], BF16)
    R3 = sba("R3", [128, 16384], BF16)
    wchunk = sba("wchunk", [128, 8192], BF16)
    wstgA = sba("wstgA", [128, 512], F32)
    wstgB = sba("wstgB", [128, 512], F32)
    consts = sba("constpool", [128, len(CONSTS)], F32)
    identb = sba("identb", [128, 128], BF16)
    masks = sba("masksb", [128, 256], BF16)
    ones = sba("onesb", [1, 128], BF16)
    qwb = sba("qwbb", [128, D], BF16)
    kwb = sba("kwbb", [128, D], BF16)
    stats = sba("stats", [128, 6 * NT], F32)
    thr = sba("thr", [128, 2 * NQ], F32)
    ropeo = sba("ropeo", [128, D], BF16)
    sa = sba("sa", [128, 1024], BF16)
    sb_ = sba("sb", [128, 1024], BF16)
    PTb = sba("PTb", [128, 2304], BF16)
    nmaxs = sba("nmaxs", [128, NH], F32)
    nmaxsb = sba("nmaxsb", [128, NH], BF16)
    nmrowA = None
    rinv = sba("rinv", [128, NH], F32)
    validb = sba("validb", [128, NT], BF16)

    for i, cval in enumerate(CONSTS):
        nc.const_aps.register(F32, cval, consts[:, i:i + 1])

    psA = nc.alloc_psum_tensor("psA", [128, 1536], F32).ap()
    psB = nc.alloc_psum_tensor("psB", [128, 1536], F32).ap()
    psC = nc.alloc_psum_tensor("psC", [128, 512], F32).ap()
    psDb = nc.alloc_psum_tensor("psD", [128, 1024], BF16).ap()
    psCb = psC.bitcast(BF16)

    xlnT = R1
    cosb = R1[:, 32768:33792]
    sinb = R1[:, 33792:34816]

    vring = R1
    o_sb_base = NKT * 2048
    kT = R2
    x2 = R2.bitcast(F32)
    qT = R3
    xqT = R3
    xres = R3.bitcast(F32)
    xtile = wchunk.bitcast(F32)
    stg32 = wchunk.bitcast(F32)
    outstg = sa.bitcast(F32)
    outstg2 = sb_.bitcast(F32)
    nmrow_half = [wstgA.bitcast(BF16), wstgB.bitcast(BF16)]
    rope_x = ropeo
    ro2stgA = ropeo.bitcast(F32)
    sums = lambda t: stats[:, 6 * t + 0:6 * t + 1]
    sumsq = lambda t: stats[:, 6 * t + 1:6 * t + 2]
    msum = lambda t: stats[:, 6 * t + 2:6 * t + 3]
    dvar = lambda t: stats[:, 6 * t + 3:6 * t + 4]
    s_of = lambda t: stats[:, 6 * t + 4:6 * t + 5]
    r_of = lambda t: stats[:, 6 * t + 5:6 * t + 6]

    WS = [Ctr(nc, "wA"), Ctr(nc, "wB"), Ctr(nc, "wC"), Ctr(nc, "wD")]
    dC = Ctr(nc, "dC")
    SP = [Ctr(nc, "spA"), Ctr(nc, "spB")]
    RNG = Ctr(nc, "rng")
    XR = [Ctr(nc, "xr0"), Ctr(nc, "xr1"), Ctr(nc, "xr2"), Ctr(nc, "xr3")]
    dO = Ctr(nc, "dO")
    dO2 = Ctr(nc, "dO2")
    ct = Ctr(nc, "ct")
    cv = Ctr(nc, "cv")
    cs = Ctr(nc, "cs")
    OWN = {"vector": cv, "scalar": cs}

    class TrPipe:
        def __init__(self):
            self.guard = [None, None]
            self.idx = 0

        def do(self, pl, src_ap, dst_ap, scale=None):
            si = self.idx % 2
            self.idx += 1
            slot = psDb[:, 0:128] if si == 0 else psCb[:, 0:128]
            if self.guard[si] is not None:
                pl.wait("tensor", cs, self.guard[si])
            tv = pl.op("tensor", lambda e, s=slot, x=src_ap: e.transpose(s, x, identb[:]),
                       sem=ct)
            pl.wait("scalar", ct, tv)
            if scale is None:
                g = pl.op("scalar", lambda e, s=slot, dd=dst_ap: e.copy(dd, s),
                          ins=[slot], out=dst_ap, sem=cs)
            else:
                g = pl.op("scalar", lambda e, s=slot, dd=dst_ap, sc=scale: e.activation(
                    out=dd, in_=s, func=AF.Copy, scale=sc), ins=[slot], out=dst_ap, sem=cs)
            self.guard[si] = g
            return g


    p = Plan(nc, OWN)
    p.dma("gpsimd", consts[:], pool_dram.ap(), dC)
    p.dma("gpsimd", identb[:], ident_dram.ap(), dC)
    p.dma("gpsimd", masks[:], masks_dram.ap(), dC)
    p.dma("gpsimd", ones[:], ones_dram.ap(), dC)
    p.dma("gpsimd", validb[:], valid_d, dC)
    dC_final = dC.v
    cos_src = bass.AP(cos_d.tensor, 0, [[64, 128], [128 * 64, 16], [1, 64]])
    sin_src = bass.AP(sin_d.tensor, 0, [[64, 128], [128 * 64, 16], [1, 64]])
    v1 = p.dma("sync", stg32[:, 0:1024], cos_src, WS[0])
    v2 = p.dma("sync", stg32[:, 1024:2048], sin_src, WS[1])
    p.wait("vector", WS[0], v1)
    vc = p.op("vector", lambda e: e.tensor_copy(out=cosb[:], in_=stg32[:, 0:1024]),
              out=cosb[:], sem=cv)
    p.wait("vector", WS[1], v2)
    p.op("vector", lambda e: e.tensor_copy(out=sinb[:], in_=stg32[:, 1024:2048]),
         out=sinb[:])
    p.wait("sync", cv, vc)
    v3 = p.dma("sync", stg32[:, 2048:4096], qwb_d, WS[0])
    p.wait("vector", WS[0], v3)
    vq = p.op("vector", lambda e: e.tensor_copy(out=qwb[:], in_=stg32[:, 2048:4096]),
              out=qwb[:], sem=cv)
    p.wait("sync", cv, vq)
    v4 = p.dma("sync", stg32[:, 2048:4096], kwb_d, WS[0])
    p.wait("vector", WS[0], v4)
    p.op("vector", lambda e: e.tensor_copy(out=kwb[:], in_=stg32[:, 2048:4096]),
         out=kwb[:])
    p.run("P0")

    def _finish_early():
        pf = Plan(nc, OWN)
        for jj in range(NQ):
            for cc in range(4):
                pf.dma("gpsimd", y_d[128 * jj:128 * (jj + 1), 512 * cc:512 * (cc + 1)],
                       x2[:, 2048 * jj + 512 * cc:2048 * jj + 512 * (cc + 1)], dO)
        pf.streams["gpsimd"].append(lambda e, v=dO.v: e.wait_ge(dO.sem, v))
        for cc in [RNG, dC, SP[0], SP[1]] + XR:
            pf.streams["gpsimd"].append(lambda e, s=cc.sem, v=cc.v: e.wait_ge(s, v))
        for cc in WS:
            pf.streams["sync"].append(lambda e, s=cc.sem, v=cc.v: e.wait_ge(s, v))
        pf.run()
        return nc
    if kphases < 1:
        return _finish_early()

    p = Plan(nc, OWN)
    for engx in ("tensor", "scalar", "vector"):
        p.wait(engx, dC, dC_final)
    tr = TrPipe()
    xslot = [xtile[:, 0:2048], xtile[:, 2048:4096]]
    cs_apply = {}
    last_tr_of_tile = {}
    dump = PTb[:, 0:2048]
    for t in range(NT):
        if t >= 2 and p1lvl >= 3:
            p.wait("sync", cs, cs_apply[t - 2])
        ld = p.dma("sync", xslot[t % 2], x_d[128 * t:128 * (t + 1), :], WS[t % 2])
        xs_ = xslot[t % 2]
        if p1lvl < 1:
            continue
        p.wait("scalar", WS[t % 2], ld)
        p.op("scalar", lambda e, x=xs_, a=sums(t): e.activation(
            out=dump, in_=x, func=AF.Copy, accum_out=a), ins=[xs_], out=(dump, sums(t)))
        st1 = p.op("scalar", lambda e, x=xs_, t=t: e.activation(
            out=dump, in_=x, func=AF.Square, accum_out=sumsq(t)),
            ins=[xs_], out=(dump, sumsq(t)), sem=cs)
        if p1lvl < 2:
            continue
        p.wait("vector", cs, st1)
        p.op("vector", lambda e, t=t: e.scalar_tensor_tensor(
            msum(t), sums(t), CPC, sums(t), ALU.mult, ALU.mult),
            ins=[sums(t)], out=msum(t))
        cvd = p.op("vector", lambda e, t=t: e.scalar_tensor_tensor(
            dvar(t), msum(t), -1.0, sumsq(t), ALU.mult, ALU.add),
            ins=[msum(t), sumsq(t)], out=dvar(t), sem=cv)
        p.wait("scalar", cv, cvd)
        css = p.op("scalar", lambda e, t=t: e.activation(
            out=s_of(t), in_=dvar(t), func=AF.Sqrt, scale=CPC, bias=LN_EPS),
            ins=[dvar(t)], out=s_of(t), sem=cs)
        p.wait("vector", cs, css)
        p.op("vector", lambda e, t=t: e.reciprocal(r_of(t), s_of(t)),
             ins=[s_of(t)], out=r_of(t))
        cvr = p.op("vector", lambda e, t=t: e.scalar_tensor_tensor(
            msum(t), sums(t), -CPC, r_of(t), ALU.mult, ALU.mult),
            ins=[sums(t), r_of(t)], out=msum(t), sem=cv)
        if p1lvl < 3:
            continue
        p.wait("scalar", cv, cvr)
        if t >= 1:
            p.wait("scalar", ct, last_tr_of_tile[t - 1])
        cs_apply[t] = p.op("scalar", lambda e, t=t, x=xs_: e.activation(
            out=ropeo[:], in_=x, func=AF.Identity, scale=r_of(t), bias=msum(t)),
            ins=[xs_, r_of(t), msum(t)], out=ropeo[:], sem=cs)
        if p1lvl < 4:
            last_tr_of_tile[t] = ct.v
            continue
        p.wait("tensor", cs, cs_apply[t])
        for d in range(16):
            dst = xlnT[:, 128 * (16 * t + d):128 * (16 * t + d + 1)]
            tr.do(p, ropeo[:, 128 * d:128 * (d + 1)], dst)
        last_tr_of_tile[t] = ct.v
    p.run("P1")
    if kphases < 2:
        return _finish_early()

    def qkv_pass(pl, col0, nchunks, tiles, dst_fn, pre_evac=None, post_evac=None,
                 w_dram=None, lhs_col=None):
        if w_dram is None:
            w_dram = wqkv_d
        if lhs_col is None:
            lhs_col = lambda t, d: 128 * (16 * t + d)
        ev = {}
        stg_guard = [None, None]
        par_guard = [None, None]
        prev_mm = None
        for ci in range(nchunks):
            c0 = col0 + 512 * ci
            if prev_mm is not None:
                pl.wait("vector", ct, prev_mm)
            for d in range(16):
                si = d % 2
                stg = wstgA if si == 0 else wstgB
                if stg_guard[si] is not None:
                    pl.wait("sync", cv, stg_guard[si])
                ld = pl.dma("sync", stg[:], w_dram[128 * d:128 * (d + 1), c0:c0 + 512],
                            WS[si])
                pl.wait("vector", WS[si], ld)
                dstw = wchunk[:, 512 * d:512 * (d + 1)]
                stg_guard[si] = pl.op("vector", lambda e, s=stg, w=dstw: e.tensor_copy(
                    out=w, in_=s), ins=[stg[:]], out=dstw, sem=cv)
            cast15 = stg_guard[1]
            pl.wait("tensor", cv, cast15)
            for ti, t in enumerate(tiles):
                par = ti % 2
                ps = (psA if par == 0 else psB)[:, 0:512]
                if par_guard[par] is not None:
                    pl.wait("tensor", cs, par_guard[par])
                ctm = None
                for d in range(16):
                    ctm = pl.op("tensor", lambda e, d=d, t=t, ps=ps: e.matmul(
                        ps, xlnT[:, lhs_col(t, d):lhs_col(t, d) + 128],
                        wchunk[:, 512 * d:512 * (d + 1)],
                        start=(d == 0), stop=(d == 15)),
                        sem=(ct if d == 15 else None))
                pl.wait("scalar", ct, ctm)
                if pre_evac is not None:
                    pre_evac(pl, t, ci)
                dst = dst_fn(t, ci)
                ev[(t, ci)] = pl.op("scalar", lambda e, ps=ps, dd=dst: e.copy(dd, ps),
                                    ins=[ps], out=dst, sem=cs)
                par_guard[par] = ev[(t, ci)]
                if post_evac is not None:
                    post_evac(pl, t, ci, ev[(t, ci)])
            prev_mm = ctm
        return ev

    def postprocess(pl, region, tiles, blkstride, base_of, w_b, sqscale, sqbias,
                    ev, last_ci, tq_of):
        rowstride = region.tensor.shape[1]

        def rows(t):
            return bass.AP(region.tensor, 128 * base_of(t),
                           [[rowstride, 128], [blkstride, 16], [1, 128]])

        def half(t, hh):
            return bass.AP(region.tensor, 128 * base_of(t) + 64 * hh,
                           [[rowstride, 128], [blkstride, 16], [1, 64]])

        for t in tiles:
            pl.wait("scalar", cs, ev[(t, last_ci)])
            pl.op("scalar", lambda e, s=rows(t), t=t: e.activation(
                out=dump, in_=s, func=AF.Square, accum_out=sumsq(t)),
                ins=[rows(t)], out=(dump, sumsq(t)))
        for t in tiles:
            pl.op("scalar", lambda e, t=t: e.activation(
                out=s_of(t), in_=sumsq(t), func=AF.Sqrt, scale=sqscale, bias=sqbias),
                ins=[sumsq(t)], out=s_of(t), sem=cs)
        cs_sqrt = cs.v
        pl.wait("vector", cs, cs_sqrt)
        for t in tiles:
            pl.op("vector", lambda e, t=t: e.reciprocal(r_of(t), s_of(t)),
                  ins=[s_of(t)], out=r_of(t), sem=cv)
        cv_r = cv.v
        pl.wait("scalar", cv, cv_r)
        for t in tiles:
            pl.op("scalar", lambda e, s=rows(t), t=t: e.activation(
                out=s, in_=s, func=AF.Copy, scale=r_of(t)),
                ins=[rows(t), r_of(t)], out=rows(t), sem=cs)
        cs_ap = cs.v
        pl.wait("vector", cs, cs_ap)
        wap = bass.AP(w_b.tensor, 0, [[2048, 128], [128, 16], [1, 128]])
        for t in tiles:
            pl.op("vector", lambda e, s=rows(t): e.tensor_mul(s, s, wap),
                  ins=[rows(t)], out=rows(t), sem=cv)
        cv_w = cv.v
        pl.wait("vector", cv, cv_w)
        tr2 = TrPipe()
        prev_tr = None
        for t in tiles:
            x1 = half(t, 0)
            x2_ = half(t, 1)
            ro1 = bass.AP(ropeo.tensor, 0, [[2048, 128], [128, 16], [1, 64]])
            ro2 = bass.AP(ropeo.tensor, 64, [[2048, 128], [128, 16], [1, 64]])
            sa_a = bass.AP(sa.tensor, 0, [[1024, 128], [64, 16], [1, 64]])
            sb_a = bass.AP(sb_.tensor, 0, [[1024, 128], [64, 16], [1, 64]])
            c64 = cosb[:, 64 * t:64 * t + 64]
            s64 = sinb[:, 64 * t:64 * t + 64]
            cos_a = bass.AP(c64.tensor, c64.offset,
                            [list(c64.ap[0]), [0, 16], list(c64.ap[1])])
            sin_a = bass.AP(s64.tensor, s64.offset,
                            [list(s64.ap[0]), [0, 16], list(s64.ap[1])])
            if prev_tr is not None:
                pl.wait("vector", ct, prev_tr)
            pl.op("vector", lambda e, a=ro1, b=x1, c=cos_a: e.tensor_mul(a, b, c),
                  ins=[x1], out=ro1)
            pl.op("vector", lambda e, a=sa_a, b=x2_, c=sin_a: e.tensor_mul(a, b, c),
                  ins=[x2_], out=sa_a)
            pl.op("vector", lambda e, a=ro2, b=x2_, c=cos_a: e.tensor_mul(a, b, c),
                  ins=[x2_], out=ro2)
            pl.op("vector", lambda e, a=sb_a, b=x1, c=sin_a: e.tensor_mul(a, b, c),
                  ins=[x1], out=sb_a)
            pl.op("vector", lambda e, a=ro1, b=sa_a: e.tensor_sub(a, a, b),
                  ins=[ro1, sa_a], out=ro1)
            cvrope = pl.op("vector", lambda e, a=ro2, b=sb_a: e.tensor_add(a, a, b),
                           ins=[ro2, sb_a], out=ro2, sem=cv)
            pl.wait("tensor", cv, cvrope)
            for h in range(16):
                dst = bass.AP(region.tensor, blkstride * h + 128 * tq_of(t),
                              [[rowstride, 128], [1, 128]])
                tr2.do(pl, ropeo[:, 128 * h:128 * (h + 1)], dst)
            prev_tr = ct.v

    p = Plan(nc, OWN)

    def k_dst(t, ci):
        return bass.AP(kT.tensor, 2048 * (4 * ci) + 128 * t,
                       [[32768, 128], [2048, 4], [1, 128]])

    ev_k = qkv_pass(p, 2048, 4, list(range(NT)), k_dst)
    postprocess(p, kT, list(range(NT)), 2048, lambda t: t, kwb, CPC, RMS_EPS,
                ev_k, 3, tq_of=lambda t: t)
    p.run("P2k")
    if kphases < 3:
        return _finish_early()

    p = Plan(nc, OWN)

    def q_dst(t, ci):
        return bass.AP(qT.tensor, 1024 * (4 * ci) + 128 * (t - 8),
                       [[16384, 128], [1024, 4], [1, 128]])

    own = list(range(8, 16))
    ev_q = qkv_pass(p, 0, 4, own, q_dst)
    postprocess(p, qT, own, 1024, lambda t: t - 8, qwb, 1.0 / 16.0, 128.0 * RMS_EPS,
                ev_q, 3, tq_of=lambda t: t - 8)
    p.run("P3q")
    if kphases < 4:
        return _finish_early()

    p = Plan(nc, OWN)
    vstg = [PTb[:, 0:512], PTb[:, 512:1024]]
    spill_guard = [None, None]

    def v_dst(t, ci):
        return vstg[(16 * ci + t) % 2][:]

    def v_pre(pl, t, ci):
        si = (16 * ci + t) % 2
        if spill_guard[si] is not None:
            pl.wait("scalar", SP[si], spill_guard[si])

    def v_post(pl, t, ci, evv):
        si = (16 * ci + t) % 2
        pl.wait("gpsimd", cs, evv)
        spill_guard[si] = pl.dma(
            "gpsimd", bass.AP(vhbm.tensor, t * 128 * D + 512 * ci, [[D, 128], [1, 512]]),
            vstg[si][:], SP[si])

    qkv_pass(p, 4096, 4, list(range(NT)), v_dst, pre_evac=v_pre, post_evac=v_post)
    p.run("P4v")
    if kphases < 5:
        return _finish_early()

    p = Plan(nc, OWN)
    p.wait("gpsimd", SP[0], SP[0].v)
    p.wait("gpsimd", SP[1], SP[1].v)
    ring_ld = {}
    for l in range(NKT):
        ring_ld[l] = p.dma("gpsimd",
                           vring[:, 2048 * (l % NKT):2048 * (l % NKT) + 2048],
                           bass.AP(vhbm.tensor, l * 128 * D, [[D, 128], [1, 2048]]), RNG)
    bank_guard = [None, None]
    pt_guard = [None, None]
    psc_guard = None
    ring_guard = {}
    o_col = lambda j, h: o_sb_base + 2048 * j + 128 * h
    for j in range(NQ):
        if j >= 1:
            sl = (j + 8) % NKT
            p.wait("gpsimd", ct, ring_guard[j - 1])
            ring_ld[j + 8] = p.dma(
                "gpsimd", vring[:, 2048 * sl:2048 * sl + 2048],
                bass.AP(vhbm.tensor, (j + 8) * 128 * D, [[D, 128], [1, 2048]]), RNG)
        red = {}
        for h in range(NH):
            par = h % 2
            ps = (psA if par == 0 else psB)
            if bank_guard[par] is not None:
                c, v = bank_guard[par]
                p.wait("tensor", c, v)
            q_ap = qT[:, 1024 * h + 128 * j:1024 * h + 128 * (j + 1)]
            ctm = None
            for (off, n) in ((0, 512), (512, 512), (1024, 128)):
                kbase = 2048 * h + 128 * j
                ctm = p.op("tensor", lambda e, ps=ps, q=q_ap, kb=kbase, o=off, n=n: e.matmul(
                    ps[:, o:o + n], q, kT[:, kb + o:kb + o + n], start=True, stop=True),
                    sem=(ct if off == 1024 else None))
            p.wait("vector", ct, ctm)
            red[h] = p.op("vector", lambda e, ps=ps, h=h: e.tensor_reduce(
                out=nmaxs[:, h:h + 1], in_=ps[:, 0:1152], axis=AX.X, op=ALU.max,
                negate=True), ins=[ps[:, 0:1152]], out=nmaxs[:, h:h + 1], sem=cv)
            bank_guard[par] = (cv, red[h])
        p.op("vector", lambda e: e.tensor_copy(out=nmaxsb[:, 0:8], in_=nmaxs[:, 0:8]),
             ins=[nmaxs[:, 0:8]], out=nmaxsb[:, 0:8])
        castb = p.op("vector", lambda e: e.tensor_copy(out=nmaxsb[:, 8:16],
                                                       in_=nmaxs[:, 8:16]),
                     ins=[nmaxs[:, 8:16]], out=nmaxsb[:, 8:16], sem=cv)
        p.wait("tensor", cv, castb)
        nm = {}
        for b in range(2):
            if b == 1:
                p.wait("tensor", cs, nm[0])
            elif j >= 1:
                p.wait("tensor", cs, nm2_prev)
            tv = None
            for h in range(8 * b, 8 * b + 8):
                tv = p.op("tensor", lambda e, h=h, b=b: e.transpose(
                    psDb[0:1, 128 * (h - 8 * b):128 * (h - 8 * b + 1)],
                    nmaxsb[:, h:h + 1], identb[:]), sem=(ct if h == 8 * b + 7 else None))
            p.wait("scalar", ct, tv)
            nm[b] = p.op("scalar", lambda e, b=b: e.copy(
                nmrow_half[b][0:1, 0:1024], psDb[0:1, 0:1024]),
                ins=[psDb[0:1, 0:1024]], out=nmrow_half[b][0:1, 0:1024], sem=cs)
        nm2 = nm[1]
        nm2_prev = nm2
        p.wait("tensor", RNG, ring_ld[j + 8])
        mk_of = {}

        def emit_ST(h):
            par = h % 2
            ps = psA if par == 0 else psB
            if bank_guard[par] is not None:
                c, v = bank_guard[par]
                p.wait("tensor", c, v)
            if h == 0:
                p.wait("tensor", cs, nm2)
            kbase = 2048 * h + 128 * j
            q_ap = qT[:, 1024 * h + 128 * j:1024 * h + 128 * (j + 1)]
            ctst = None
            for l in range(NKT):
                p.op("tensor", lambda e, ps=ps, l=l, kb=kbase, q=q_ap: e.matmul(
                    ps[:, 128 * l:128 * (l + 1)],
                    kT[:, kb + 128 * l:kb + 128 * (l + 1)],
                    q, start=True, stop=False))
                nmr = nmrow_half[h // 8][0:1, 128 * (h % 8):128 * (h % 8 + 1)]
                ctst = p.op("tensor", lambda e, ps=ps, l=l, r=nmr: e.matmul(
                    ps[:, 128 * l:128 * (l + 1)], ones[:], r, start=False, stop=True),
                    sem=(ct if l == NKT - 1 else None))
            PT = PTb[:, 1152 * par:1152 * (par + 1)]
            p.wait("scalar", ct, ctst)
            if pt_guard[par] is not None:
                c, v = pt_guard[par]
                p.wait("scalar", c, v)
            expv = p.op("scalar", lambda e, ps=ps, PT=PT: e.activation(
                out=PT, in_=ps[:, 0:1152], func=AF.Exp), ins=[ps[:, 0:1152]], out=PT,
                sem=cs)
            bank_guard[par] = (cs, expv)
            p.wait("vector", cs, expv)
            p.op("vector", lambda e, PT=PT: e.tensor_mul(
                PT[:, 0:128], PT[:, 0:128], masks[:, 0:128]),
                ins=[PT[:, 0:128]], out=PT[:, 0:128])
            mk_of[h] = p.op("vector", lambda e, PT=PT: e.tensor_mul(
                PT[:, 1024:1152], PT[:, 1024:1152], masks[:, 128:256]),
                ins=[PT[:, 1024:1152]], out=PT[:, 1024:1152], sem=cv)

        def emit_PV(h):
            nonlocal psc_guard
            par = h % 2
            PT = PTb[:, 1152 * par:1152 * (par + 1)]
            p.wait("tensor", cv, mk_of[h])
            if psc_guard is not None:
                p.wait("tensor", cs, psc_guard)
            for l in range(NKT):
                sl = (j + l) % NKT
                p.op("tensor", lambda e, PT=PT, l=l, sl=sl, h=h: e.matmul(
                    psC[:, 0:128], PT[:, 128 * l:128 * (l + 1)],
                    vring[:, 2048 * sl + 128 * h:2048 * sl + 128 * (h + 1)],
                    start=(l == 0), stop=(l == NKT - 1)))
            ctpv = None
            for l in range(NKT):
                ctpv = p.op("tensor", lambda e, PT=PT, l=l, j=j: e.matmul(
                    psC[:, 128:129], PT[:, 128 * l:128 * (l + 1)],
                    validb[:, j + l:j + l + 1], start=(l == 0), stop=(l == NKT - 1)),
                    sem=(ct if l == NKT - 1 else None))
            pt_guard[par] = (ct, ctpv)
            p.wait("vector", ct, ctpv)
            rcv = p.op("vector", lambda e, h=h: e.reciprocal(
                rinv[:, h:h + 1], psC[:, 128:129]), ins=[psC[:, 128:129]],
                out=rinv[:, h:h + 1], sem=cv)
            p.wait("scalar", cv, rcv)
            oc = o_col(j, h)
            psc_guard = p.op("scalar", lambda e, h=h, oc=oc: e.activation(
                out=R1[:, oc:oc + 128], in_=psC[:, 0:128], func=AF.Copy,
                scale=rinv[:, h:h + 1]), ins=[psC[:, 0:128], rinv[:, h:h + 1]],
                out=R1[:, oc:oc + 128], sem=cs)

        emit_ST(0)
        for h in range(1, NH):
            emit_ST(h)
            emit_PV(h - 1)
        emit_PV(NH - 1)
        ring_guard[j] = ct.v
    p.run("P5attn")
    if kphases < 6:
        return _finish_early()

    p = Plan(nc, OWN)
    tr6 = TrPipe()
    for j in range(NQ):
        for h in range(NH):
            c0 = o_sb_base + 2048 * j + 128 * h
            tr6.do(p, R1[:, c0:c0 + 128], R1[:, c0:c0 + 128])
    cs_oT = cs.v
    p.wait("tensor", cs, cs_oT)
    xres_guard = [None, None, None, None]
    stg_guard6 = [None, None]
    par_guard6 = [None, None]
    prev_mm6 = None
    for ci in range(4):
        c0 = 512 * ci
        if prev_mm6 is not None:
            p.wait("vector", ct, prev_mm6)
        for d in range(16):
            si = d % 2
            stg = wstgA if si == 0 else wstgB
            if stg_guard6[si] is not None:
                p.wait("sync", cv, stg_guard6[si])
            ld = p.dma("sync", stg[:], wo_d[128 * d:128 * (d + 1), c0:c0 + 512], WS[si])
            p.wait("vector", WS[si], ld)
            dstw = wchunk[:, 512 * d:512 * (d + 1)]
            stg_guard6[si] = p.op("vector", lambda e, s=stg, w=dstw: e.tensor_copy(
                out=w, in_=s), ins=[stg[:]], out=dstw, sem=cv)
        p.wait("tensor", cv, stg_guard6[1])
        for j in range(NQ):
            si = j % 4
            ps = (psA if j % 2 == 0 else psB)[:, 0:512]
            if xres_guard[si] is not None:
                p.wait("gpsimd", cv, xres_guard[si])
            xl = p.dma("gpsimd", xres[:, 512 * si:512 * (si + 1)],
                       x_d[128 * (8 + j):128 * (9 + j), c0:c0 + 512], XR[si])
            if par_guard6[j % 2] is not None:
                p.wait("tensor", cv, par_guard6[j % 2])
            ctm = None
            for h in range(NH):
                c0o = o_sb_base + 2048 * j + 128 * h
                ctm = p.op("tensor", lambda e, ps=ps, c0o=c0o, h=h: e.matmul(
                    ps, R1[:, c0o:c0o + 128], wchunk[:, 512 * h:512 * (h + 1)],
                    start=(h == 0), stop=(h == NH - 1)),
                    sem=(ct if h == NH - 1 else None))
            p.wait("vector", ct, ctm)
            p.wait("vector", XR[si], xl)
            dstx = x2[:, 2048 * j + c0:2048 * j + c0 + 512]
            evj = p.op("vector", lambda e, ps=ps, si=si, dd=dstx: e.tensor_add(
                dd, ps, xres[:, 512 * si:512 * (si + 1)]),
                ins=[ps, xres[:, 512 * si:512 * (si + 1)]], out=dstx, sem=cv)
            xres_guard[si] = evj
            par_guard6[j % 2] = evj
        prev_mm6 = ctm
    p.run("P6wo")
    if kphases < 7:
        return _finish_early()

    p = Plan(nc, OWN)
    th_of = lambda j: thr[:, 2 * j:2 * j + 1]
    tl_of = lambda j: thr[:, 2 * j + 1:2 * j + 2]
    tr7 = TrPipe()
    cvx_prev = None
    for j in range(NQ):
        xt = x2[:, 2048 * j:2048 * (j + 1)]
        if cvx_prev is not None:
            p.wait("scalar", cv, cvx_prev)
        p.op("scalar", lambda e, x=xt, a=sums(j): e.activation(
            out=dump, in_=x, func=AF.Copy, accum_out=a), ins=[xt], out=(dump, sums(j)))
        stv = p.op("scalar", lambda e, x=xt, a=sumsq(j): e.activation(
            out=dump, in_=x, func=AF.Square, accum_out=a), ins=[xt],
            out=(dump, sumsq(j)), sem=cs)
        p.wait("vector", cs, stv)
        p.op("vector", lambda e, j=j: e.scalar_tensor_tensor(
            msum(j), sums(j), CPC, sums(j), ALU.mult, ALU.mult),
            ins=[sums(j)], out=msum(j))
        cvd = p.op("vector", lambda e, j=j: e.scalar_tensor_tensor(
            dvar(j), msum(j), -1.0, sumsq(j), ALU.mult, ALU.add),
            ins=[msum(j), sumsq(j)], out=dvar(j), sem=cv)
        p.wait("scalar", cv, cvd)
        csq = p.op("scalar", lambda e, j=j: e.activation(
            out=s_of(j), in_=dvar(j), func=AF.Sqrt, scale=CPC, bias=LN_EPS),
            ins=[dvar(j)], out=s_of(j), sem=cs)
        p.wait("vector", cs, csq)
        p.op("vector", lambda e, j=j: e.tensor_scalar_mul(r_of(j), sums(j), CPC),
             ins=[sums(j)], out=r_of(j))
        p.op("vector", lambda e, j=j: e.scalar_tensor_tensor(
            th_of(j), s_of(j), 0.33, r_of(j), ALU.mult, ALU.add),
            ins=[s_of(j), r_of(j)], out=th_of(j))
        cvt = p.op("vector", lambda e, j=j: e.scalar_tensor_tensor(
            tl_of(j), s_of(j), -0.33, r_of(j), ALU.mult, ALU.add),
            ins=[s_of(j), r_of(j)], out=tl_of(j), sem=cv)
        p.wait("vector", cv, cvt)
        g_d = PTb[:, 0:2048]
        l_d = ropeo[:, 0:2048]
        p.op("vector", lambda e, x=xt, j=j: e.tensor_single_scalar(
            g_d, x, th_of(j), ALU.is_gt), ins=[xt, th_of(j)], out=g_d)
        p.op("vector", lambda e, x=xt, j=j: e.tensor_single_scalar(
            l_d, x, tl_of(j), ALU.is_lt), ins=[xt, tl_of(j)], out=l_d)
        xq_t = xqT[:, 2048 * j:2048 * (j + 1)]
        cvx = p.op("vector", lambda e, a=xq_t, g=g_d, l=l_d: e.tensor_sub(a, g, l),
                   ins=[g_d, l_d], out=xq_t, sem=cv)
        cvx_prev = cvx
        p.wait("tensor", cv, cvx)
        for d in range(16):
            c0 = 2048 * j + 128 * d
            tr7.do(p, xqT[:, c0:c0 + 128], xqT[:, c0:c0 + 128], scale=0.5)
    cs_xqT = cs.v
    p.run("P7ln2")
    if kphases < 8:
        return _finish_early()

    p = Plan(nc, OWN)
    tmpA = [PTb[:, 0:512], PTb[:, 1024:1536]]
    tmpB = [PTb[:, 512:1024], PTb[:, 1536:2048]]
    stgW = [wstgA, ro2stgA[:, 0:512]]
    stgR = [wstgB, ro2stgA[:, 512:1024]]
    out_guard = [None, None]
    built_guard = [None, None]
    par_guard8 = [None, None]
    prev_mm8 = None
    for ci in range(4):
        c0 = 512 * ci
        if prev_mm8 is not None:
            p.wait("vector", ct, prev_mm8)
        for d in range(16):
            si = d % 2
            if built_guard[si] is not None:
                p.wait("sync", cv, built_guard[si])
                p.wait("scalar", cv, built_guard[si])
            ldw = p.dma("sync", stgW[si][:], wfT_d[128 * d:128 * (d + 1), c0:c0 + 512],
                        WS[si])
            ldr = p.dma("sync", stgR[si][:], rrT_d[128 * d:128 * (d + 1), c0:c0 + 512],
                        WS[2 + si])
            p.wait("scalar", WS[si], ldw)
            p.op("scalar", lambda e, s=stgW[si], tA=tmpA[si]: e.activation(
                out=tA, in_=s[:], func=AF.Sign, bias=-0.2), ins=[stgW[si][:]], out=tmpA[si])
            sgn = p.op("scalar", lambda e, s=stgW[si], tB=tmpB[si]: e.activation(
                out=tB, in_=s[:], func=AF.Sign, bias=0.2), ins=[stgW[si][:]], out=tmpB[si],
                sem=cs)
            p.wait("vector", cs, sgn)
            p.op("vector", lambda e, tA=tmpA[si], tB=tmpB[si]: e.tensor_add(tA, tA, tB),
                 ins=[tmpA[si], tmpB[si]], out=tmpA[si])
            p.wait("vector", WS[2 + si], ldr)
            dstw = wchunk[:, 512 * d:512 * (d + 1)]
            built_guard[si] = p.op(
                "vector", lambda e, s=stgR[si], w=dstw, tA=tmpA[si]: e.scalar_tensor_tensor(
                    w, s[:], 3.3333333, tA, ALU.mult, ALU.add),
                ins=[stgR[si][:], tmpA[si]], out=dstw, sem=cv)
        p.wait("tensor", cv, built_guard[1])
        if ci == 0:
            p.wait("tensor", cs, cs_xqT)
        for j in range(NQ):
            ps = (psA if j % 2 == 0 else psB)[:, 0:512]
            if par_guard8[j % 2] is not None:
                p.wait("tensor", cv, par_guard8[j % 2])
            ctm = None
            for d in range(16):
                c0x = 2048 * j + 128 * d
                ctm = p.op("tensor", lambda e, ps=ps, c0x=c0x, d=d: e.matmul(
                    ps, xqT[:, c0x:c0x + 128], wchunk[:, 512 * d:512 * (d + 1)],
                    start=(d == 0), stop=(d == 15)),
                    sem=(ct if d == 15 else None))
            p.wait("vector", ct, ctm)
            osl = (8 * ci + j) % 2
            ostg = outstg if osl == 0 else outstg2
            osem = dO if osl == 0 else dO2
            if out_guard[osl] is not None:
                p.wait("vector", osem, out_guard[osl])
            dstx = x2[:, 2048 * j + c0:2048 * j + c0 + 512]
            evo = p.op("vector", lambda e, ps=ps, xx=dstx, o=ostg: e.scalar_tensor_tensor(
                o[:, 0:512], ps, 0.3, xx, ALU.mult, ALU.add),
                ins=[ps, dstx], out=ostg[:, 0:512], sem=cv)
            par_guard8[j % 2] = evo
            p.wait("gpsimd", cv, evo)
            out_guard[osl] = p.dma("gpsimd", y_d[128 * j:128 * (j + 1), c0:c0 + 512],
                                   ostg[:, 0:512], osem)
        prev_mm8 = ctm
    for cc in [dO, dO2, RNG, dC, SP[0], SP[1]] + XR:
        p.streams["gpsimd"].append(lambda e, s=cc.sem, v=cc.v: e.wait_ge(s, v))
    for cc in WS:
        p.streams["sync"].append(lambda e, s=cc.sem, v=cc.v: e.wait_ge(s, v))
    p.run("P8ffn")

    return nc


_NC_CACHE = {}


def kernel(L0_attn_hidden_states, L0_attn_Wqkv, L0_attn_q_norm_w, L0_attn_k_norm_w,
           L0_attn_Wo, L0_ffn_weight, L0_ffn_rram_variation):
    X = np.ascontiguousarray(np.asarray(L0_attn_hidden_states, dtype=np.float32)
                             .reshape(S, D))
    wqkv = np.ascontiguousarray(np.asarray(L0_attn_Wqkv, dtype=np.float32))
    wo = np.ascontiguousarray(np.asarray(L0_attn_Wo, dtype=np.float32))
    qw = np.asarray(L0_attn_q_norm_w, dtype=np.float32)
    kw = np.asarray(L0_attn_k_norm_w, dtype=np.float32)
    wfT = np.ascontiguousarray(np.asarray(L0_ffn_weight, dtype=np.float32).T)
    rrT = np.ascontiguousarray(np.asarray(L0_ffn_rram_variation, dtype=np.float32).T)

    qwb = np.ascontiguousarray(np.broadcast_to(qw[None, :], (128, D)))
    kwb = np.ascontiguousarray(np.broadcast_to(kw[None, :], (128, D)))

    inv_freq = 1.0 / THETA ** (np.arange(0, 64, dtype=np.float64) / 64.0)

    if "nc" not in _NC_CACHE:
        _NC_CACHE["nc"] = build_nc()
    nc = _NC_CACHE["nc"]

    in_maps = []
    for c in range(NCORES):
        t0 = 1024 * c - 1024
        if c == 0:
            x_all = np.concatenate([np.zeros((1024, D), np.float32),
                                    X[0:1024]], axis=0)
        else:
            x_all = X[t0:t0 + 2048]
        pos = np.arange(t0, t0 + 2048, dtype=np.float64)
        pos = np.clip(pos, 0, None)
        fr = pos[:, None] * inv_freq[None, :]
        cosT = np.cos(fr).astype(np.float32)
        sinT = np.sin(fr).astype(np.float32)
        valid = np.ones((128, NT), dtype=bfdt)
        if c == 0:
            valid[:, 0:8] = 0
        in_maps.append({
            "x_all": np.ascontiguousarray(x_all),
            "wqkv": wqkv, "wo": wo, "wfT": wfT, "rrT": rrT,
            "qwb": qwb, "kwb": kwb,
            "cosT": cosT, "sinT": sinT, "valid": valid,
        })

    res = run_bass_kernel_spmd(nc, in_maps, core_ids=list(range(NCORES)))
    out = np.concatenate([res.results[c]["y"] for c in range(NCORES)], axis=0)
    return out.reshape(1, S, D)
